# version 61
# baseline (speedup 1.0000x reference)
import contextlib
import os
import numpy as np
import concourse.bass as bass
import concourse.mybir as mybir
from concourse.bass_utils import run_bass_kernel_spmd

F32 = mybir.dt.float32
F16 = mybir.dt.float16
I16 = mybir.dt.int16
U16 = mybir.dt.uint16
I8 = mybir.dt.int8
AF = mybir.ActivationFunctionType
ALU = mybir.AluOpType
AX = mybir.AxisListType

NCORE = 8
SEQ = 16384
T = SEQ // NCORE
D = 1024
NS = 64
H = 3072
NT = T // 128
ND = D // 128
NO = H // 128
OG = 384
HG = 128
EPS = 1e-5
QI = 1.0 / 32767.0
RAWS = 0.125

_CACHED = {}


class Prog:
    def __init__(self):
        self.ops = {e: [] for e in "yvsgt"}
        self.cnt = {e: 0 for e in "yvsgt"}
        self.dma = {}
        self.maxw = {e: {} for e in "yvsgt"}
        self.ES = {}
        self.DS = {}
        self.CS = None
        self.ncoll = 0

    def wait(self, e, tok):
        kind = tok[0]
        if e == "t" and kind == "c" and tok[1] == "t":
            return
        if kind == "k":
            v = tok[1]
            key = ("k", "k")
            sem = self.CS
        else:
            _, src, v = tok
            key = (kind, src)
            sem = self.ES[src] if kind == "c" else self.DS[src]
        if self.maxw[e].get(key, 0) >= v:
            return
        self.maxw[e][key] = v
        self.ops[e].append(lambda eng, sem=sem, v=v: eng.wait_ge(sem, v))

    def op(self, e, f, deps=()):
        for t in deps:
            self.wait(e, t)
        self.cnt[e] += 1
        v = self.cnt[e]

        def run(eng, f=f, e=e):
            f(eng).then_inc(self.ES[e], 1)
        self.ops[e].append(run)
        return ("c", e, v)

    def dmaop(self, q, f, deps=(), ch=None):
        assert ch is not None
        cur = self.dma.get(ch, 0)
        if cur > 0:
            self.wait(q, ("d", ch, cur))
        for t in deps:
            self.wait(q, t)
        self.dma[ch] = cur + 16
        v = self.dma[ch]

        def run(eng, f=f, ch=ch):
            f(eng).then_inc(self.DS[ch], 16)
        self.ops[q].append(run)
        return ("d", ch, v)

    def coll(self, kind, alu, ins, outs, deps=()):
        for t in deps:
            self.wait("g", t)
        self.ncoll += 1
        v = self.ncoll

        def run(eng, kind=kind, alu=alu, ins=ins, outs=outs):
            eng.collective_compute(kind, alu, replica_groups=[list(range(NCORE))],
                                   ins=ins, outs=outs).then_inc(self.CS, 1)
        self.ops["g"].append(run)
        return ("k", v)


class _StopEmission(Exception):
    pass


def build():
    BIS = float(os.environ.get("BISECT", "9"))
    nc = bass.Bass("TRN2", target_bir_lowering=False)

    u_in = nc.dram_tensor("u_in", [T, D], F32, kind="ExternalInput").ap()
    ldr = nc.dram_tensor("ldr", [128, 8], F32, kind="ExternalInput").ap()
    dgm = nc.dram_tensor("dgm", [128, 4096], F32, kind="ExternalInput").ap()
    lar = nc.dram_tensor("lar", [NS], F32, kind="ExternalInput").ap()
    btr = nc.dram_tensor("btr", [D, NS], F32, kind="ExternalInput").ap()
    ctr = nc.dram_tensor("ctr", [NS, D], F32, kind="ExternalInput").ap()
    wg_nat = nc.dram_tensor("wg_nat", [OG, D], F32, kind="ExternalInput").ap()
    wu_nat = nc.dram_tensor("wu_nat", [OG, D], F32, kind="ExternalInput").ap()
    wd_nat = nc.dram_tensor("wd_nat", [HG, H], F32, kind="ExternalInput").ap()
    cexp = nc.dram_tensor("cexp", [8], F32, kind="ExternalInput").ap()
    cmask = nc.dram_tensor("cmask", [8], F32, kind="ExternalInput").ap()
    idm = nc.dram_tensor("idm", [128, 128], F16, kind="ExternalInput").ap()
    out_d = nc.dram_tensor("out_d", [T, D], F32, kind="ExternalOutput").ap()

    SHARD = OG * D * 2 + HG * H
    cc_in = nc.dram_tensor("cc_in", [1, SHARD // 2], F32).ap()
    cc_out = nc.dram_tensor("cc_out", [NCORE, OG * D // 2], F32, addr_space="Shared").ap()
    cc_out2 = nc.dram_tensor("cc_out2", [NCORE, (SHARD - OG * D) // 2], F32, addr_space="Shared").ap()
    ags_in = nc.dram_tensor("ags_in", [1, NS], F32).ap()
    ags_out = nc.dram_tensor("ags_out", [NCORE, NS], F32, addr_space="Shared").ap()
    ar_in = [nc.dram_tensor(f"ar{i}_in", [1, 2], F32).ap() for i in range(4)]
    ar_out = [nc.dram_tensor(f"ar{i}_out", [1, 2], F32, addr_space="Shared").ap() for i in range(4)]
    bb = [nc.dram_tensor(f"bb{i}", [1, 4], F32).ap() for i in range(4)]
    x1_sp = nc.dram_tensor("x1_sp", [T, D], F32).ap()
    up_sp = nc.dram_tensor("up_sp", [NO, 128, T], F16).ap()
    sv_b = nc.dram_tensor("sv_b", [1, 128 * 32 * 2], F32).ap()
    a_b = nc.dram_tensor("a_b", [1, NS], F32).ap()

    cc_in16 = cc_in.bitcast(F16)
    cc_out16 = cc_out.bitcast(F16)
    cc_out216 = cc_out2.bitcast(F16)

    p = Prog()

    with contextlib.ExitStack() as st:
        E = st.enter_context

        ARENA = E(nc.sbuf_tensor("ARENA", [128, NO * T], F16))
        AR32 = ARENA[:, :].bitcast(F32)
        XT = E(nc.sbuf_tensor("XT", [128, ND * T], F16, is_transpose_dest=True))
        XT32 = XT[:, :].bitcast(F32)
        XTI = XT[:, :].bitcast(I16)
        WBUF = E(nc.sbuf_tensor("WBUF", [128, ND * H], F16))
        WB32 = WBUF[:, :].bitcast(F32)
        WB8 = WBUF[:, :].bitcast(I8)
        RB = E(nc.sbuf_tensor("RB", [128, 10240], F16))
        RB32 = RB[:, :].bitcast(F32)
        RBI = RB[:, :].bitcast(I16)
        RBU = RB[:, :].bitcast(U16)

        def upr(k):
            s = k % 4
            if s == 3:
                return AR32[:, 8192:10240]
            return AR32[:, s * 2048:(s + 1) * 2048]

        dgq = ARENA[:, 12288:16384]

        def x1t(g):
            return AR32[:, 10240 + (g % 2) * 2048:10240 + (g % 2 + 1) * 2048]
        BU = AR32[0:NS, 14336:14336 + T]
        XS0 = AR32[0:NS, 14336 + T:14336 + 2 * T]
        ABC = AR32[0:NS, 14336 + 2 * T:14336 + 3 * T]
        IOT = AR32[0:NS, 14336 + 3 * T:14336 + 4 * T]
        def yri(g):
            return AR32[:, 14336 + (g % 2) * 2048:14336 + (g % 2 + 1) * 2048]

        def ynri(g):
            return AR32[:, 18432 + (g % 2) * 2048:18432 + (g % 2 + 1) * 2048]
        sqs_a = AR32[:, 22528:23552]
        xsf = ARENA[0:NS, 47104:47104 + T]

        def big(oc):
            return ARENA[:, oc * T:(oc + 1) * T]

        def rlr(oc):
            return XT[:, (oc % 2) * T:(oc % 2 + 1) * T]

        def gnr(oc):
            return XTI[:, (2 + oc % 2) * T:(3 + oc % 2) * T]

        def oir(oc):
            return XTI[:, (oc % 2) * T:(oc % 2 + 1) * T]

        def uir(oc):
            return XTI[:, (6 + oc % 2) * T:(7 + oc % 2) * T]

        def y2r(i):
            return XT32[:, 4 * T + (i % 2) * 512:4 * T + (i % 2 + 1) * 512]
        def y2p(g):
            return XT32[:, (g % 2) * T:(g % 2 + 1) * T]

        def outp(g):
            return y2p(g)

        def yn4p(g):
            return XT32[:, 2 * T + (g % 2) * T:2 * T + (g % 2 + 1) * T]
        sqs_n = RB32[:, 0:D]

        def x1p(g):
            return RB32[:, 1024 + (g % 2) * T:1024 + (g % 2 + 1) * T]

        def xn1r(g):
            return RB[:, (g % 2) * T:(g % 2 + 1) * T]

        def xn3r(g):
            return RB[:, (2 + g % 2) * T:(3 + g % 2) * T]
        def gir(oc):
            if oc % 2 == 0:
                return RBI[:, 0:T]
            return RBI[:, 8192:8192 + T]

        def sgr(oc):
            return RB[:, T + (oc % 2) * T:T + (oc % 2 + 1) * T]

        def sur(oc):
            return RBU[:, T + (oc % 2) * T:T + (oc % 2 + 1) * T]

        def evr(oc):
            return RB[:, 3 * T:4 * T]

        def natr(j):
            return WB32[:, (j % 2) * D:(j % 2 + 1) * D]
        def dqtmp(j):
            return WB32[:, (2 + j % 2) * 1024:(3 + j % 2) * 1024]
        def wq8(j):
            return WB8[:, (16 + j % 2) * 1024:(17 + j % 2) * 1024]
        def wdq(j):
            return WBUF[:, (9 + j % 2) * 1024:(10 + j % 2) * 1024]
        def wsht(j):
            return WBUF[:, (11 + j % 2) * 1024:(12 + j % 2) * 1024]
        def sqd(j):
            return sqd_[:, (j % 2) * 32:(j % 2 + 1) * 32]
        def sqi(j):
            return sqi_[:, (j % 2) * 32:(j % 2 + 1) * 32]

        stm = E(nc.sbuf_tensor("stm", [128, 20 * NT], F32))
        gmax = E(nc.sbuf_tensor("gmax", [128, NO], F32))
        umax = E(nc.sbuf_tensor("umax", [128, NO], F32))
        g2m = E(nc.sbuf_tensor("g2m", [128, NO], F32))
        pmax = E(nc.sbuf_tensor("pmax", [128, NO], F32))
        xmax = E(nc.sbuf_tensor("xmax", [128, NT], F32))
        red1 = E(nc.sbuf_tensor("red1", [128, 8], F32))
        red0 = E(nc.sbuf_tensor("red0", [1, 8], F32))
        SC = E(nc.sbuf_tensor("SC", [1, 32], F32))
        BC = E(nc.sbuf_tensor("BC", [128, 8], F32))

        ldt = E(nc.sbuf_tensor("ldt", [128, 8], F32))
        dlt = E(nc.sbuf_tensor("dlt", [128, 8], F32))
        dsc = E(nc.sbuf_tensor("dsc", [128, 16], F32))
        arow = E(nc.sbuf_tensor("arow", [1, NS], F32))
        a64 = E(nc.sbuf_tensor("a64", [NS, 1], F32))
        lnac = E(nc.sbuf_tensor("lnac", [NS, 1], F32))
        carry = E(nc.sbuf_tensor("carry", [NS, 1], F32))
        bt16 = E(nc.sbuf_tensor("bt16", [128, ND * NS], F16))
        dbt = E(nc.sbuf_tensor("dbt", [128, ND * NS], F16))
        ct16 = E(nc.sbuf_tensor("ct16", [NS, D], F16))
        arb = E(nc.sbuf_tensor("arb", [128, NS], F32))
        abar = E(nc.sbuf_tensor("abar", [128, NS], F32))
        abar2 = E(nc.sbuf_tensor("abar2", [128, NS], F32))
        ones1 = E(nc.sbuf_tensor("ones1", [128, 1], F32))
        onesr = E(nc.sbuf_tensor("onesr", [1, 128], F32))
        cb = E(nc.sbuf_tensor("cb", [128, 2], F32))
        smT = E(nc.sbuf_tensor("smT", [NS, 8], F32))
        ebx = E(nc.sbuf_tensor("ebx", [NS, 8], F32))
        mbx = E(nc.sbuf_tensor("mbx", [NS, 8], F32))
        cw1 = E(nc.sbuf_tensor("cw1", [NS, 8], F32))
        cw2 = E(nc.sbuf_tensor("cw2", [NS, 8], F32))
        sqd_ = E(nc.sbuf_tensor("sqd", [128, 64], F32))
        sqi_ = E(nc.sbuf_tensor("sqi", [128, 64], F32))
        idn = E(nc.sbuf_tensor("idn", [128, 128], F16))

        ps_pre = contextlib.ExitStack()
        ps_bu = ps_pre.enter_context(nc.psum_tensor("ps_bu", [NS, 1024], F32))
        ps_y = ps_pre.enter_context(nc.psum_tensor("ps_y", [128, T], F32))
        ps_tr = [ps_pre.enter_context(nc.psum_tensor(f"ps_tr{k}", [128, 1024], F16)) for k in range(2)]

        for e in "yvsgt":
            p.ES[e] = E(nc.semaphore(f"ES_{e}"))
        CHANNELS = (["ua", "ub", "uc", "ud", "wl", "cs", "cs2", "sp", "us", "ur0", "ur1", "xr", "ot", "ot2", "tr0", "tr1",
                     "gs", "gx", "q0", "q1"] + [f"t{c}" for c in range(8)])
        for ch in CHANNELS:
            p.DS[ch] = E(nc.semaphore(f"DS_{ch}"))
        p.CS = E(nc.semaphore("CS"))

        block = E(nc.Block())

        try:
            t_idn = p.dmaop("y", lambda s: s.dma_start(out=idn[:, :], in_=idm[:, :]), ch="wl")
            _tr_state = {0: None, 1: None, "n": 0}

            def pe_transpose8(src_1024, dst_view, deps, ev_eng="s"):
                k = _tr_state["n"] % 2
                _tr_state["n"] += 1
                war = [_tr_state[k]] if _tr_state[k] is not None else []
                tt = None
                for cc in range(8):
                    tt = p.op("t", lambda t, src_1024=src_1024, k=k, cc=cc: t.transpose(
                        ps_tr[k][:, cc * 128:(cc + 1) * 128],
                        src_1024[:, cc * 128:(cc + 1) * 128], idn[:, :]),
                        deps=(list(deps) + war + [t_idn]) if cc == 0 else [tt])
                src_ps = ps_tr[k][:, :].rearrange("p (c q) -> p c q", q=128)
                if ev_eng == "s":
                    ev = p.op("s", lambda s, dst_view=dst_view, src_ps=src_ps: s.copy(
                        dst_view, src_ps), deps=[tt])
                else:
                    ev = p.op("v", lambda v, dst_view=dst_view, src_ps=src_ps: v.tensor_copy(
                        out=dst_view, in_=src_ps), deps=[tt])
                _tr_state[k] = ev
                return ev
            t_ldr = p.dmaop("y", lambda s: s.dma_start(out=ldt[:, :], in_=ldr[:, :]), ch="wl")
            t_bt = p.dmaop("g", lambda g: g.dma_start(
                out=bt16[:, :].rearrange("p (c n) -> p c n", n=NS),
                in_=btr.rearrange("(c p) n -> p c n", p=128)), ch="gx")
            t_ct = p.dmaop("g", lambda g: g.dma_start(out=ct16[:, :], in_=ctr[:, :]), ch="gx")
            t_dgq = p.dmaop("g", lambda g: g.dma_start(out=dgq[:, :], in_=dgm[:, :]), ch="gs")
            t_lar = p.dmaop("g", lambda g: g.dma_start(
                out=arb[:, :], in_=lar.rearrange("(o n) -> o n", o=1).broadcast_to([128, NS])), ch="gx")
            t_ebx = p.dmaop("g", lambda g: g.dma_start(
                out=ebx[:, :], in_=cexp.rearrange("(o m) -> o m", o=1).broadcast_to([NS, 8])), ch="gx")
            t_mbx = p.dmaop("g", lambda g: g.dma_start(
                out=mbx[:, :], in_=cmask.rearrange("(o m) -> o m", o=1).broadcast_to([NS, 8])), ch="gs")
            t_ones = p.op("g", lambda g: g.memset(ones1[:, :], 1.0))
            t_onesr = p.op("g", lambda g: g.memset(onesr[:, :], 1.0))
            t_cb0 = p.op("g", lambda g: g.memset(cb[:, 0:1], EPS))
            t_cb = p.op("g", lambda g: g.memset(cb[:, 1:2], 1.0), deps=[t_cb0])
            t_iota = p.op("g", lambda g: g.iota(IOT, pattern=[[1, T]], base=1,
                                                channel_multiplier=0,
                                                allow_small_or_imprecise_dtypes=True))

            d_abs, d_en = dsc[:, 0:8], dsc[:, 8:16]
            b1 = p.op("s", lambda s: s.activation(out=d_abs, in_=ldt[:, :], func=AF.Abs), deps=[t_ldr])
            b2 = p.op("s", lambda s: s.activation(out=d_en, in_=d_abs, func=AF.Exp, scale=-1.0), deps=[b1])
            b3 = p.op("s", lambda s: s.activation(out=d_en, in_=d_en, func=AF.Ln, bias=cb[:, 1:2]), deps=[b2, t_cb])
            b4 = p.op("v", lambda v: v.tensor_scalar(out=dlt[:, :], in0=ldt[:, :], scalar1=0.0,
                                                     scalar2=None, op0=ALU.max), deps=[t_ldr])
            b5 = p.op("v", lambda v: v.tensor_tensor(out=dlt[:, :], in0=dlt[:, :], in1=d_en,
                                                     op=ALU.add), deps=[b3, b4])
            b6 = p.op("s", lambda s: s.activation(out=arb[:, :], in_=arb[:, :], func=AF.Exp),
                      deps=[t_lar])
            b7 = p.op("v", lambda v: v.tensor_scalar(out=arb[:, :], in0=arb[:, :], scalar1=-1.0,
                                                     scalar2=None, op0=ALU.mult), deps=[b6])
            t_dbt = [p.op("v", lambda v, c=c: v.tensor_scalar(
                out=dbt[:, c * NS:(c + 1) * NS], in0=bt16[:, c * NS:(c + 1) * NS],
                scalar1=dlt[:, c:c + 1], scalar2=None, op0=ALU.mult), deps=[t_bt, b5])
                for c in range(ND)]
            ps_a = ps_bu[0:1, 0:NS]
            t_am = []
            for c in range(ND):
                te = p.op("v", lambda v, c=c: v.tensor_scalar(
                    out=abar[:, :], in0=arb[:, :], scalar1=dlt[:, c:c + 1], scalar2=None,
                    op0=ALU.mult), deps=[b7, b5] + t_am[-1:])
                tx = p.op("s", lambda s: s.activation(out=abar2[:, :], in_=abar[:, :], func=AF.Exp),
                          deps=[te])
                t_am.append(p.op("t", lambda t, c=c: t.matmul(
                    ps_a, lhsT=ones1[:, :], rhs=abar2[:, :], start=(c == 0), stop=(c == ND - 1)),
                    deps=[tx, t_ones]))
            t_arw = p.op("v", lambda v: v.tensor_scalar(out=arow[:, :], in0=ps_a, scalar1=1.0 / D,
                                                        scalar2=None, op0=ALU.mult), deps=[t_am[-1]])
            ta1 = p.dmaop("g", lambda g: g.dma_start(out=a_b[0:1, :], in_=arow[:, :]), deps=[t_arw], ch="gs")
            ta2 = p.dmaop("g", lambda g: g.dma_start(
                out=a64[:, :], in_=a_b[0:1, :].rearrange("o (n x) -> (o n) x", x=1),
                allow_slow_non_contiguous=True), deps=[ta1], ch="gs")
            t_ln = p.op("s", lambda s: s.activation(out=lnac[:, :], in_=a64[:, :], func=AF.Ln),
                        deps=[ta2])
            t_lnc = p.op("v", lambda v: v.tensor_scalar(out=lnac[:, :], in0=lnac[:, :],
                                                        scalar1=-1e30, scalar2=None, op0=ALU.max),
                         deps=[t_ln])

            if BIS <= 1:
                raise _StopEmission()
            tiles = []
            for k in range(3):
                tiles.append(("g", wg_nat[k * 128:(k + 1) * 128, :], 0, OG, k * 128))
            for k in range(3):
                tiles.append(("u", wu_nat[k * 128:(k + 1) * 128, :], OG * D, OG, k * 128))
            for k in range(3):
                tiles.append(("d", wd_nat[:, k * D:(k + 1) * D], 2 * OG * D + k * D * HG, HG, 0))

            dq_store = []
            dq_last = []
            DQ = {}

            def dq_tile(j):
                wname, src_, base, ostr, coff = tiles[j]
                pv = DQ.get(j - 2, {})
                pw = DQ.get(j - 1, {})
                cur = {}
                tl = p.dmaop("y", lambda s, j=j, src_=src_: s.dma_start(out=natr(j), in_=src_),
                             deps=([pv["t4a"]] if pv else []), ch="wl")
                t1 = p.op("v", lambda v, j=j: v.tensor_reduce(
                    out=sqd(j), in_=natr(j).rearrange("p (b k) -> p b k", k=32),
                    axis=AX.X, op=ALU.max, apply_absolute_value=True),
                    deps=[tl] + ([pv["t5"]] if pv else []))
                t2 = p.op("v", lambda v, j=j: v.tensor_scalar(
                    out=sqd(j), in0=sqd(j), scalar1=1.0 / 7.0, scalar2=1e-12,
                    op0=ALU.mult, op1=ALU.max), deps=[t1])
                t3 = p.op("v", lambda v, j=j: v.reciprocal(out=sqi(j), in_=sqd(j)),
                          deps=[t2] + ([pv["t4a"]] if pv else []))
                t4a = p.op("g", lambda g, j=j: g.tensor_tensor(
                    out=dqtmp(j).rearrange("p (b k) -> p b k", k=32),
                    in0=natr(j).rearrange("p (b k) -> p b k", k=32),
                    in1=sqi(j).unsqueeze(2).broadcast_to([128, 32, 32]), op=ALU.mult),
                    deps=[t3, tl] + ([pv["t4"]] if pv else []))
                cur["t4a"] = t4a
                t4 = p.op("v", lambda v, j=j: v.tensor_copy(out=wq8(j), in_=dqtmp(j)),
                          deps=[t4a] + ([pv["t5"]] if pv else []))
                cur["t4"] = t4
                t5 = p.op("v", lambda v, j=j: v.tensor_tensor(
                    out=wdq(j).rearrange("p (b k) -> p b k", k=32),
                    in0=wq8(j).rearrange("p (b k) -> p b k", k=32),
                    in1=sqd(j).unsqueeze(2).broadcast_to([128, 32, 32]), op=ALU.mult),
                    deps=[t4] + ([pv["tr"]] if pv else []))
                cur["t5"] = t5
                tr = pe_transpose8(wdq(j), wsht(j).rearrange("p (c q) -> p c q", q=128),
                                   deps=[t5] + ([pv["st"]] if pv else []), ev_eng="v")
                cur["tr"] = tr
                st_tok = p.dmaop("g", lambda g, j=j, base=base, ostr=ostr, coff=coff: g.dma_start(
                    out=bass.AP(cc_in16.tensor, base + coff,
                                [[ostr, 128], [128 * ostr, 8], [1, 128]]),
                    in_=wsht(j).rearrange("p (c q) -> p c q", q=128)), deps=[tr],
                    ch="cs" if j % 2 == 0 else "cs2")
                cur["st"] = st_tok
                dq_store.append(st_tok)
                dq_last.extend([t5, tr, st_tok])
                DQ[j] = cur

            _agw = [None]
            t_agw2 = [None]

            def emit_gate_ag():
                _agw[0] = p.coll("AllGather", ALU.bypass, [cc_in[:, 0:OG * D // 2]],
                                 [cc_out[:, :]], deps=dq_store[0:3])

            if BIS <= 2:
                raise _StopEmission()
            def w_reload(w, c, deps):
                if w == 0:
                    src = cc_out16[:, 0:OG * D].rearrange("r (d o) -> d r o", o=OG)
                    src = src[c * 128:(c + 1) * 128, :, :]
                    dst = WBUF[:, c * H:(c + 1) * H].rearrange("p (r o) -> p r o", o=OG)
                elif w == 1:
                    src = cc_out216[:, 0:OG * D].rearrange("r (d o) -> d r o", o=OG)
                    src = src[c * 128:(c + 1) * 128, :, :]
                    dst = WBUF[:, c * H:(c + 1) * H].rearrange("p (r o) -> p r o", o=OG)
                else:
                    base = OG * D
                    src = cc_out216[:, base:base + HG * H].rearrange("r (q h) -> q r h", h=HG)
                    src = src[c * 128:(c + 1) * 128, :, :]
                    dst = WBUF[:, c * D:(c + 1) * D].rearrange("p (r h) -> p r h", h=HG)
                tok = p.dmaop("y", lambda s, src=src, dst=dst: s.dma_start(out=dst, in_=src),
                              deps=deps, ch="wl")
                return tok

            def ln_stats(i, src_tile, grp, deps, sq_scratch, war_sq):
                base = grp * 4 * NT
                m_sl = stm[:, base + i:base + i + 1]
                r_sl = stm[:, base + NT + i:base + NT + i + 1]
                s_sl = stm[:, base + 2 * NT + i:base + 2 * NT + i + 1]
                q_sl = stm[:, base + 3 * NT + i:base + 3 * NT + i + 1]
                t1 = p.op("v", lambda v: v.tensor_reduce(out=s_sl, in_=src_tile, axis=AX.X,
                                                         op=ALU.add), deps=deps)
                t2 = p.op("s", lambda s: s.activation(out=sq_scratch, in_=src_tile, func=AF.Square,
                                                      accum_out=q_sl), deps=deps + war_sq)
                t3 = p.op("v", lambda v: v.tensor_scalar(out=m_sl, in0=s_sl, scalar1=1.0 / D,
                                                         scalar2=None, op0=ALU.mult), deps=[t1])
                t4 = p.op("v", lambda v: v.tensor_scalar(out=r_sl, in0=m_sl, scalar1=m_sl,
                                                         scalar2=None, op0=ALU.mult), deps=[t3])
                t5 = p.op("v", lambda v: v.scalar_tensor_tensor(out=r_sl, in0=q_sl, scalar=1.0 / D,
                                                                in1=r_sl, op0=ALU.mult,
                                                                op1=ALU.subtract), deps=[t2, t4])
                t6 = p.op("s", lambda s: s.activation(out=r_sl, in_=r_sl, func=AF.Ln, bias=cb[:, 0:1]),
                          deps=[t5, t_cb])
                t7 = p.op("s", lambda s: s.activation(out=r_sl, in_=r_sl, func=AF.Exp, scale=-0.5),
                          deps=[t6])
                return t7, m_sl, r_sl, t2

            def pair_slots(g, grp):
                base = grp * 5 * NT
                i0 = 2 * g
                return (stm[:, base + i0:base + i0 + 2],
                        stm[:, base + NT + i0:base + NT + i0 + 2],
                        stm[:, base + 2 * NT + i0:base + 2 * NT + i0 + 2],
                        stm[:, base + 3 * NT + i0:base + 3 * NT + i0 + 2],
                        stm[:, base + 4 * NT + i0:base + 4 * NT + i0 + 2])

            sqch = [None]

            def ln_pair(g, src_pair, grp, deps, war_sq):
                m_sl, r_sl, s_sl, q_sl, b_sl = pair_slots(g, grp)
                t1 = p.op("v", lambda v: v.tensor_reduce(
                    out=s_sl, in_=src_pair.rearrange("p (w d) -> p w d", d=D),
                    axis=AX.X, op=ALU.add), deps=deps)
                tq0 = p.op("s", lambda s: s.activation(out=sqs_a, in_=src_pair[:, 0:D],
                                                       func=AF.Square, accum_out=q_sl[:, 0:1]),
                           deps=deps + ([sqch[0]] if sqch[0] is not None else []))
                tq1 = p.op("s", lambda s: s.activation(out=sqs_a, in_=src_pair[:, D:2 * D],
                                                       func=AF.Square, accum_out=q_sl[:, 1:2]),
                           deps=deps + [tq0])
                sqch[0] = tq1
                t3 = p.op("v", lambda v: v.tensor_scalar(out=m_sl, in0=s_sl, scalar1=1.0 / D,
                                                         scalar2=None, op0=ALU.mult), deps=[t1])
                t4 = p.op("v", lambda v: v.tensor_tensor(out=r_sl, in0=m_sl, in1=m_sl,
                                                         op=ALU.mult), deps=[t3])
                t5 = p.op("v", lambda v: v.scalar_tensor_tensor(out=r_sl, in0=q_sl, scalar=1.0 / D,
                                                                in1=r_sl, op0=ALU.mult,
                                                                op1=ALU.subtract), deps=[tq1, t4])
                t6 = p.op("s", lambda s: s.activation(out=r_sl, in_=r_sl, func=AF.Ln,
                                                      bias=cb[:, 0:1]), deps=[t5, t_cb])
                t7 = p.op("s", lambda s: s.activation(out=r_sl, in_=r_sl, func=AF.Exp,
                                                      scale=-0.5), deps=[t6])
                t8 = p.op("v", lambda v: v.scalar_tensor_tensor(out=b_sl, in0=m_sl, scalar=-1.0,
                                                                in1=r_sl, op0=ALU.mult,
                                                                op1=ALU.mult), deps=[t7])
                return t8, m_sl, r_sl, b_sl

            def cross_max(src_ap, deps, r1slot, r0slot):
                ta = p.op("v", lambda v: v.tensor_reduce(out=red1[:, r1slot:r1slot + 1], in_=src_ap,
                                                         axis=AX.X, op=ALU.max), deps=deps)
                tb = p.op("g", lambda g: g.tensor_reduce(out=red0[0:1, r0slot:r0slot + 1],
                                                         in_=red1[:, r1slot:r1slot + 1],
                                                         axis=AX.C, op=ALU.max), deps=[ta])
                return tb

            NP2 = NT // 2
            t_tr1 = [[None] * ND for _ in range(NT)]
            t_xn1 = {}
            last_sq = []
            lst1 = {}
            t_up = {}
            UCH = ["ua", "ub", "uc", "ud"]
            for g in range(3):
                h0 = p.dmaop("y", lambda s, g=g: s.dma_start(
                    out=upr(g).rearrange("p (w d) -> p w d", d=D)[:, 0:1, :],
                    in_=u_in[g * 256:g * 256 + 128, :].rearrange("(o p) d -> p o d", p=128)),
                    deps=[], ch=UCH[(2 * g) % 4])
                h1 = p.dmaop("y", lambda s, g=g: s.dma_start(
                    out=upr(g).rearrange("p (w d) -> p w d", d=D)[:, 1:2, :],
                    in_=u_in[g * 256 + 128:(g + 1) * 256, :].rearrange("(o p) d -> p o d", p=128)),
                    deps=[], ch=UCH[(2 * g + 1) % 4])
                t_up[g] = (h0, h1)
            def emitDA(g):
                ud = list(t_up[g]) if isinstance(t_up[g][0], tuple) else [t_up[g]]
                tst = ln_pair(g, upr(g), 0, ud, last_sq[-1:])[0]
                last_sq.append(tst)
                lst1[g] = tst

            def emitDB(g):
                tst = lst1[g]
                for w in range(2):
                    i = 2 * g + w
                    war = [t_tr1[i - 4][0] for _ in range(1)] if i >= 4 else []
                    tap = p.op("s", lambda s, g=g, w=w, i=i: s.activation(
                        out=xn1r(g)[:, w * D:(w + 1) * D], in_=upr(g)[:, w * D:(w + 1) * D],
                        func=AF.Identity, scale=pair_slots(g, 0)[1][:, w:w + 1],
                        bias=pair_slots(g, 0)[4][:, w:w + 1]), deps=[tst] + war)
                    t_xn1[i] = tap
                    dstv = XT[:, :].rearrange("p (c t) -> p c t", t=T)[:, :, i * 128:(i + 1) * 128]
                    ev1 = pe_transpose8(xn1r(g)[:, w * D:(w + 1) * D], dstv, deps=[tap], ev_eng="s")
                    for c in range(ND):
                        t_tr1[i][c] = ev1
                if g + 3 < NP2:
                    dd = [t_xn1[2 * g - 1], lst1[g - 1]] if g >= 1 else []
                    t_up[g + 3] = p.dmaop("y", lambda s, g=g: s.dma_start(
                        out=upr(g + 3).rearrange("p (w d) -> p w d", d=D),
                        in_=u_in[(g + 3) * 256:(g + 4) * 256, :].rearrange("(w p) d -> p w d", p=128)),
                        deps=dd, ch=UCH[(g + 3) % 4])
                if g < 8:
                    dq_tile(g)
                if g == 2:
                    emit_gate_ag()

            emitDA(0)
            for g in range(1, NP2):
                emitDA(g)
                emitDB(g - 1)
            emitDB(NP2 - 1)
            dq_tile(8)

            t_bu_mm = []
            t_bu_ev = []
            for h in range(2):
                dps_ev = [t_bu_ev[-1]] if t_bu_ev else []
                for c in range(ND):
                    for tq2 in range(2):
                        tq = 2 * h + tq2
                        dps = [t_tr1[4 * tq + j][c] for j in range(4)] + [t_dbt[c], t_arw] + dps_ev
                        t_bu_mm.append(p.op("t", lambda t, c=c, tq=tq, tq2=tq2: t.matmul(
                            ps_bu[:, tq2 * 512:(tq2 + 1) * 512],
                            lhsT=dbt[:, c * NS:(c + 1) * NS],
                            rhs=XT[:, c * T + tq * 512:c * T + (tq + 1) * 512],
                            start=(c == 0), stop=(c == ND - 1)), deps=dps))
                t_bu_ev.append(p.op("v", lambda v, h=h: v.tensor_copy(
                    out=BU[:, h * 1024:(h + 1) * 1024], in_=ps_bu[:, :]), deps=[t_bu_mm[-1]]))
            t_bu = t_bu_ev[-1]

            t_abc = p.op("v", lambda v: v.tensor_scalar(out=ABC, in0=IOT, scalar1=0.0,
                                                        scalar2=a64[:, 0:1], op0=ALU.mult,
                                                        op1=ALU.add), deps=[t_iota, ta2])
            t_scan = p.op("v", lambda v: v.tensor_tensor_scan(out=XS0, data0=ABC, data1=BU,
                                                              initial=0.0, op0=ALU.mult,
                                                              op1=ALU.add), deps=[t_abc, t_bu])
            t_sks = p.dmaop("g", lambda g: g.dma_start(out=ags_in[0:1, :], in_=XS0[:, T - 1:T],
                                                       allow_slow_non_contiguous=True),
                            deps=[t_scan], ch="gs")
            t_agS = p.coll("AllGather", ALU.bypass, [ags_in[:, :]], [ags_out[:, :]], deps=[t_sks])
            t_agw2[0] = p.coll("AllGather", ALU.bypass,
                               [cc_in[:, OG * D // 2:]], [cc_out2[:, :]], deps=dq_store)
            t_sm = p.dmaop("g", lambda g: g.dma_start(out=smT[:, :],
                                                      in_=ags_out.rearrange("m n -> n m"),
                                                      allow_slow_non_contiguous=True),
                           deps=[t_agS], ch="gs")
            c1_ = p.op("v", lambda v: v.tensor_scalar(out=cw1[:, :], in0=ebx[:, :],
                                                      scalar1=lnac[:, 0:1], scalar2=None,
                                                      op0=ALU.mult), deps=[t_ebx, t_lnc])
            c2_ = p.op("s", lambda s: s.activation(out=cw1[:, :], in_=cw1[:, :], func=AF.Exp),
                       deps=[c1_])
            c3_ = p.op("v", lambda v: v.tensor_tensor(out=cw2[:, :], in0=cw1[:, :], in1=mbx[:, :],
                                                      op=ALU.mult), deps=[c2_, t_mbx])
            c4_ = p.op("v", lambda v: v.tensor_tensor(out=cw2[:, :], in0=cw2[:, :], in1=smT[:, :],
                                                      op=ALU.mult), deps=[c3_, t_sm])
            t_cry = p.op("v", lambda v: v.tensor_reduce(out=carry[:, :], in_=cw2[:, :], axis=AX.X,
                                                        op=ALU.add), deps=[c4_])
            t_apw = p.op("s", lambda s: s.activation(out=ABC, in_=IOT, func=AF.Exp,
                                                     scale=lnac[:, 0:1]), deps=[t_scan, t_lnc])
            t_xs = p.op("v", lambda v: v.scalar_tensor_tensor(out=BU, in0=ABC, scalar=carry[:, 0:1],
                                                              in1=XS0, op0=ALU.mult, op1=ALU.add),
                        deps=[t_apw, t_cry, t_bu, t_scan])
            t_xsf = p.op("v", lambda v: v.tensor_copy(out=xsf, in_=BU), deps=[t_xs])

            if BIS <= 3.5:
                raise _StopEmission()
            t_tr3 = [[None] * ND for _ in range(NT)]
            t_x1sp = {}
            t_xm = []
            t_ty = {}
            t_tyn = {}
            t_tx1 = {}
            t_txn3 = {}
            for g in range(3):
                ku = 8 + g
                pv = 4 + g
                t_up[ku] = p.dmaop("y", lambda s, g=g: s.dma_start(
                    out=upr(8 + g).rearrange("p (w d) -> p w d", d=D),
                    in_=u_in[g * 256:(g + 1) * 256, :].rearrange("(w p) d -> p w d", p=128)),
                    deps=[t_xn1[2 * pv + 1], lst1[pv]], ch=UCH[(8 + g) % 4])

            def u2p(g):
                return upr(8 + g)

            def emitA(g):
                m2_sl, r2_sl, s2_sl, q2_sl = pair_slots(g, 1)[0:4]
                evw = {}
                for w in range(2):
                    i = 2 * g + w
                    last = None
                    for nh in range(2):
                        for c in range(4 * nh, 4 * nh + 4):
                            dps = [last] if last is not None else (
                                [t_dgq, t_xsf, t_ct, t_tr1[i][0]]
                                + ([t_evC[(g - 1, w)], t_evQ[(g - 1, w)]] if g >= 1 else []))
                            last = p.op("t", lambda t, g=g, w=w, c=c, nh=nh, i=i: t.matmul(
                                ps_y[:, w * D + nh * 512:w * D + (nh + 1) * 512],
                                lhsT=XT[:, c * T + i * 128:c * T + (i + 1) * 128],
                                rhs=dgq[:, c * 512:(c + 1) * 512],
                                start=(c == 4 * nh), stop=False), deps=dps)
                        last = p.op("t", lambda t, g=g, w=w, nh=nh, i=i: t.matmul(
                            ps_y[:, w * D + nh * 512:w * D + (nh + 1) * 512],
                            lhsT=xsf[:, i * 128:(i + 1) * 128],
                            rhs=ct16[:, nh * 512:(nh + 1) * 512],
                            start=False, stop=True), deps=[last])
                    t_dgl[(g, w)] = last
                    evw[w] = last
                for w in range(2):
                    t_evC[(g, w)] = p.op("s", lambda s, g=g, w=w, s2_sl=s2_sl: s.activation(
                        out=yri(g)[:, w * D:(w + 1) * D], in_=ps_y[:, w * D:(w + 1) * D],
                        func=AF.Copy, accum_out=s2_sl[:, w:w + 1]),
                        deps=[evw[w]]
                        + ([t_tyn[(g - 2, w)]] if g >= 2 else [t_xs]))
                    tq = p.op("s", lambda s, g=g, w=w, q2_sl=q2_sl: s.activation(
                        out=sqs_a, in_=ps_y[:, w * D:(w + 1) * D],
                        func=AF.Square, accum_out=q2_sl[:, w:w + 1]),
                        deps=[evw[w]] + ([sqch[0]] if sqch[0] is not None else []))
                    sqch[0] = tq
                    t_evQ[(g, w)] = tq
                t3 = p.op("v", lambda v, m2_sl=m2_sl, s2_sl=s2_sl: v.tensor_scalar(
                    out=m2_sl, in0=s2_sl, scalar1=1.0 / D, scalar2=None, op0=ALU.mult),
                    deps=[t_evC[(g, 0)], t_evC[(g, 1)]])
                t4 = p.op("v", lambda v, r2_sl=r2_sl, m2_sl=m2_sl: v.tensor_tensor(
                    out=r2_sl, in0=m2_sl, in1=m2_sl, op=ALU.mult), deps=[t3])
                t5 = p.op("v", lambda v, r2_sl=r2_sl, q2_sl=q2_sl: v.scalar_tensor_tensor(
                    out=r2_sl, in0=q2_sl, scalar=1.0 / D, in1=r2_sl, op0=ALU.mult,
                    op1=ALU.subtract), deps=[t_evQ[(g, 0)], t_evQ[(g, 1)], t4])
                t6 = p.op("s", lambda s, r2_sl=r2_sl: s.activation(out=r2_sl, in_=r2_sl,
                                                                   func=AF.Ln, bias=cb[:, 0:1]),
                          deps=[t5, t_cb])
                t_st2[g] = p.op("s", lambda s, r2_sl=r2_sl: s.activation(
                    out=r2_sl, in_=r2_sl, func=AF.Exp, scale=-0.5), deps=[t6])

            def emitB1(g):
                m2_sl, r2_sl = pair_slots(g, 1)[0:2]
                for w in range(2):
                    t_tyn[(g, w)] = p.op("v", lambda v, g=g, w=w, m2_sl=m2_sl, r2_sl=r2_sl:
                                         v.tensor_scalar(
                        out=ynri(g)[:, w * D:(w + 1) * D], in0=yri(g)[:, w * D:(w + 1) * D],
                        scalar1=m2_sl[:, w:w + 1], scalar2=r2_sl[:, w:w + 1],
                        op0=ALU.subtract, op1=ALU.mult),
                        deps=[t_st2[g]] + ([t_tx1[g - 2]] if g >= 2 else []))
                t_tx1[g] = p.op("v", lambda v, g=g: v.tensor_tensor(
                    out=x1t(g), in0=u2p(g), in1=ynri(g), op=ALU.add),
                    deps=[t_tyn[(g, 0)], t_tyn[(g, 1)], t_up[8 + g]]
                    + ([t_x1sp[g - 2], t_txn3[g - 2]] if g >= 2 else []))
                if g + 3 < NP2:
                    ku = 8 + g + 3
                    pv = ku - 4
                    dd = [t_xn1[2 * pv + 1], lst1[pv]] if pv < 8 else [t_tx1[pv - 8]]
                    t_up[ku] = p.dmaop("y", lambda s, g=g, ku=ku: s.dma_start(
                        out=upr(ku).rearrange("p (w d) -> p w d", d=D),
                        in_=u_in[(g + 3) * 256:(g + 4) * 256, :]
                            .rearrange("(w p) d -> p w d", p=128),
                        ), deps=dd, ch=UCH[ku % 4])

            def emitB2n(g):
                m1_sl = pair_slots(g, 0)[0]
                r3_sl, s3_sl, q3_sl = pair_slots(g, 2)[1:4]
                tqa = p.op("s", lambda s, g=g, q3_sl=q3_sl: s.activation(
                    out=sqs_a, in_=x1t(g)[:, 0:D], func=AF.Square, accum_out=q3_sl[:, 0:1]),
                    deps=[t_tx1[g]] + ([sqch[0]] if sqch[0] is not None else []))
                tqb = p.op("s", lambda s, g=g, q3_sl=q3_sl: s.activation(
                    out=sqs_a, in_=x1t(g)[:, D:2 * D], func=AF.Square, accum_out=q3_sl[:, 1:2]),
                    deps=[t_tx1[g], tqa])
                sqch[0] = tqb
                t4 = p.op("v", lambda v, r3_sl=r3_sl, m1_sl=m1_sl: v.tensor_tensor(
                    out=r3_sl, in0=m1_sl, in1=m1_sl, op=ALU.mult), deps=[t_tx1[g]])
                t5 = p.op("v", lambda v, r3_sl=r3_sl, q3_sl=q3_sl: v.scalar_tensor_tensor(
                    out=r3_sl, in0=q3_sl, scalar=1.0 / D, in1=r3_sl, op0=ALU.mult,
                    op1=ALU.subtract), deps=[tqb, t4])
                t6 = p.op("s", lambda s, r3_sl=r3_sl: s.activation(out=r3_sl, in_=r3_sl,
                                                                   func=AF.Ln, bias=cb[:, 0:1]),
                          deps=[t5, t_cb])
                tst3 = p.op("s", lambda s, r3_sl=r3_sl: s.activation(
                    out=r3_sl, in_=r3_sl, func=AF.Exp, scale=-0.5), deps=[t6])
                for w in range(2):
                    war_e = ([t_tr3[2 * (g - 2) + w][0], t_xm[g - 2]] if g >= 2 else [])
                    t3w = p.op("v", lambda v, g=g, w=w, m1_sl=m1_sl, r3_sl=r3_sl:
                               v.tensor_scalar(
                        out=xn3r(g)[:, w * D:(w + 1) * D], in0=x1t(g)[:, w * D:(w + 1) * D],
                        scalar1=m1_sl[:, w:w + 1], scalar2=r3_sl[:, w:w + 1],
                        op0=ALU.subtract, op1=ALU.mult), deps=[tst3] + war_e)
                    t_tw3[(g, w)] = t3w
                    t_txn3[g] = t3w
                t_xm.append(p.op("v", lambda v, g=g: v.tensor_reduce(
                    out=xmax[:, 2 * g:2 * g + 2],
                    in_=xn3r(g).rearrange("p (w d) -> p w d", d=D),
                    axis=AX.X, op=ALU.max, apply_absolute_value=True),
                    deps=[t_txn3[g]]))
                t_x1sp[g] = p.dmaop("y", lambda s, g=g: s.dma_start(
                    out=x1_sp[g * 256:(g + 1) * 256, :]
                        .rearrange("(w p) d -> p w d", p=128),
                    in_=x1t(g).rearrange("p (w d) -> p w d", d=D)),
                    deps=[t_tx1[g], tst3], ch="sp")

            def emitB2(g):
                for w in range(2):
                    i = 2 * g + w
                    dstv3 = XT[:, :].rearrange("p (c t) -> p c t", t=T)[:, :, i * 128:(i + 1) * 128]
                    ev3 = p.dmaop("y", lambda s, g=g, w=w, dstv3=dstv3: s.dma_start(
                        out=dstv3, in_=xn3r(g)[:, w * D:(w + 1) * D], transpose=True),
                        deps=[t_tw3[(g, w)], t_bu_mm[-1], t_dgl[(g, w)]],
                        ch="tr0" if w == 0 else "tr1")
                    for c in range(ND):
                        t_tr3[i][c] = ev3

            t_st2 = {}
            t_evC = {}
            t_evQ = {}
            t_dgl = {}
            t_tw3 = {}
            emitA(0)
            emitA(1)
            emitB1(0)
            emitA(2)
            emitB1(1)
            emitB2n(0)
            for g in range(3, NP2):
                emitA(g)
                emitB2(g - 3)
                emitB1(g - 1)
                emitB2n(g - 2)
            emitB2(NP2 - 3)
            emitB1(NP2 - 1)
            emitB2n(NP2 - 2)
            emitB2(NP2 - 2)
            emitB2n(NP2 - 1)
            emitB2(NP2 - 1)

            if BIS <= 4:
                raise _StopEmission()
            tmx = cross_max(xmax[:, :], t_xm, 0, 0)
            a0s = p.dmaop("g", lambda g: g.dma_start(out=ar_in[0][0:1, 0:1], in_=red0[0:1, 0:1]),
                          deps=[tmx], ch="gs")
            a0 = p.coll("AllReduce", ALU.max, [ar_in[0][:, 0:1]], [ar_out[0][:, 0:1]], deps=[a0s])
            t_mx = p.dmaop("g", lambda g: g.dma_start(out=SC[0:1, 0:1], in_=ar_out[0][0:1, 0:1]),
                           deps=[a0], ch="gs")
            t_sx = p.op("v", lambda v: v.tensor_scalar(out=SC[0:1, 1:2], in0=SC[0:1, 0:1],
                                                       scalar1=QI, scalar2=1e-12, op0=ALU.mult,
                                                       op1=ALU.max), deps=[t_mx])
            t_c1 = p.op("v", lambda v: v.reciprocal(out=SC[0:1, 2:3], in_=SC[0:1, 1:2]), deps=[t_sx])
            bm0 = p.op("t", lambda t: t.matmul(
                ps_y[:, 0:1], lhsT=onesr[:, :], rhs=SC[0:1, 2:3], start=True, stop=True),
                deps=[t_c1, t_onesr, t_evQ[(NP2 - 1, 0)], t_evQ[(NP2 - 1, 1)],
                      t_evC[(NP2 - 1, 0)], t_evC[(NP2 - 1, 1)]])
            t_bc1 = p.op("s", lambda s: s.activation(out=BC[:, 0:1], in_=ps_y[:, 0:1],
                                                     func=AF.Copy), deps=[bm0])

            t_xq = []
            for c in range(ND):
                sc_ = gir(0)
                deps0 = [t_tr3[i][c] for i in range(NT)] + [t_bc1] + t_xq[-1:]
                ta = p.op("v", lambda v, c=c, sc_=sc_: v.tensor_scalar(
                    out=sc_, in0=XT[:, c * T:(c + 1) * T], scalar1=BC[:, 0:1], scalar2=None,
                    op0=ALU.mult), deps=deps0)
                tb = p.op("v", lambda v, c=c, sc_=sc_: v.tensor_copy(
                    out=XT[:, c * T:(c + 1) * T], in_=sc_), deps=[ta])
                t_xq.append(tb)

            if BIS <= 5:
                raise _StopEmission()
            ps_pre.close()
            ps_mm = [st.enter_context(nc.psum_tensor(f"ps_mm{k}", [128, T], F32)) for k in range(2)]

            p.wait("y", _agw[0])
            t_wgr = [w_reload(0, c, dq_last) for c in range(ND)]
            if BIS <= 5.2:
                raise _StopEmission()
            NOg = NO if BIS > 5.5 else 2
            if BIS <= 5.45:
                NOg = 2

            t_gev = {}
            t_gmx = {}
            last = None
            for oc in range(NOg):
                psum = ps_mm[oc % 2]
                war = [t_gev[oc - 2], t_gmx[oc - 2]] if oc >= 2 else []
                for c in range(ND):
                    for tq in range(4):
                        dps = ([t_xq[c], t_wgr[c]] + war) if tq == 0 else []
                        last = p.op("t", lambda t, oc=oc, c=c, tq=tq, psum=psum: t.matmul(
                            psum[:, tq * 512:(tq + 1) * 512],
                            lhsT=WBUF[:, c * H + oc * 128:c * H + (oc + 1) * 128],
                            rhs=XT[:, c * T + tq * 512:c * T + (tq + 1) * 512],
                            start=(c == 0), stop=(c == ND - 1)), deps=dps + ([last] if last else []))
                war_ar = [t_x1sp[NP2 - 1]]
                if BIS <= 5.3:
                    t_gev[oc] = p.op("v", lambda v, oc=oc, psum=psum: v.tensor_copy(
                        out=big(oc), in_=psum[:, :]), deps=[last] + war_ar)
                    t_gmx[oc] = t_gev[oc]
                    continue
                t_gev[oc] = p.op("s", lambda s, oc=oc, psum=psum: s.activation(
                    out=big(oc), in_=psum[:, :], func=AF.Copy, scale=RAWS),
                    deps=[last] + war_ar)
                if BIS <= 5.4:
                    t_gmx[oc] = t_gev[oc]
                    continue
                t_gmx[oc] = p.op("v", lambda v, oc=oc: v.tensor_reduce(
                    out=gmax[:, oc:oc + 1], in_=big(oc), axis=AX.X, op=ALU.max,
                    apply_absolute_value=True), deps=[t_gev[oc]])

            if BIS <= 5.5:
                raise _StopEmission()
            tg = cross_max(gmax[:, :], [t_gmx[oc] for oc in range(NO)], 1, 1)
            a1s = p.dmaop("g", lambda g: g.dma_start(out=ar_in[1][0:1, 0:1], in_=red0[0:1, 1:2]),
                          deps=[tg], ch="gs")
            a1 = p.coll("AllReduce", ALU.max, [ar_in[1][:, 0:1]], [ar_out[1][:, 0:1]], deps=[a1s])
            t_mg = p.dmaop("g", lambda g: g.dma_start(out=SC[0:1, 3:4], in_=ar_out[1][0:1, 0:1]),
                           deps=[a1], ch="gs")
            q1 = p.op("v", lambda v: v.tensor_tensor(out=SC[0:1, 16:17], in0=SC[0:1, 3:4],
                                                     in1=SC[0:1, 1:2], op=ALU.mult),
                      deps=[t_mg, t_sx])
            t_sg = p.op("v", lambda v: v.tensor_scalar(out=SC[0:1, 4:5], in0=SC[0:1, 16:17],
                                                       scalar1=QI / RAWS, scalar2=1e-12, op0=ALU.mult,
                                                       op1=ALU.max), deps=[q1])
            q2 = p.op("v", lambda v: v.reciprocal(out=SC[0:1, 17:18], in_=SC[0:1, 4:5]), deps=[t_sg])
            q3 = p.op("v", lambda v: v.tensor_tensor(out=SC[0:1, 18:19], in0=SC[0:1, 17:18],
                                                     in1=SC[0:1, 1:2], op=ALU.mult), deps=[q2])
            t_c2 = p.op("v", lambda v: v.tensor_scalar(out=SC[0:1, 5:6], in0=SC[0:1, 18:19],
                                                       scalar1=1.0 / RAWS, scalar2=None,
                                                       op0=ALU.mult), deps=[q3])
            b1s = p.dmaop("g", lambda g: g.dma_start(out=bb[1][0:1, 0:1], in_=SC[0:1, 4:5]),
                          deps=[t_sg], ch="gs")
            b1b = p.dmaop("g", lambda g: g.dma_start(out=bb[1][0:1, 1:2], in_=SC[0:1, 5:6]),
                          deps=[t_c2], ch="gs")
            t_bc2 = p.dmaop("g", lambda g: g.dma_start(
                out=BC[:, 1:3], in_=bb[1][0:1, 0:2].broadcast_to([128, 2])), deps=[b1s, b1b], ch="gs")

            if BIS <= 6:
                raise _StopEmission()
            p.wait("y", t_agw2[0])
            t_wur = [w_reload(1, c, [t_gev[NO - 1], t_gmx[NO - 1]]) for c in range(ND)]
            t_uev = {}
            t_umx = {}
            t_usp = {}
            t_g2 = {}
            t_g2m = {}
            for oc in range(NO):
                psum = ps_mm[oc % 2]
                war = ([t_uev[oc - 2], t_umx[oc - 2]] if oc >= 2
                       else [t_gev[NO - 2 + oc], t_gmx[NO - 2 + oc]])
                for c in range(ND):
                    for tq in range(4):
                        dps = ([t_wur[c]] + war) if tq == 0 else []
                        last = p.op("t", lambda t, oc=oc, c=c, tq=tq, psum=psum: t.matmul(
                            psum[:, tq * 512:(tq + 1) * 512],
                            lhsT=WBUF[:, c * H + oc * 128:c * H + (oc + 1) * 128],
                            rhs=XT[:, c * T + tq * 512:c * T + (tq + 1) * 512],
                            start=(c == 0), stop=(c == ND - 1)), deps=dps + [last])
                war1 = [t_g2[oc - 2]] if oc >= 2 else t_xq[-1:]
                ta = p.op("s", lambda s, oc=oc: s.activation(
                    out=gir(oc), in_=big(oc), func=AF.Copy, scale=BC[:, 2:3]),
                    deps=[t_gev[oc], t_bc2] + war1)
                tb = p.op("s", lambda s, oc=oc: s.activation(
                    out=sgr(oc), in_=gir(oc), func=AF.Sigmoid, scale=BC[:, 1:2]),
                    deps=[ta, t_bc2] + ([t_g2[oc - 2]] if oc >= 2 else []))
                tc = p.op("v", lambda v, oc=oc: v.tensor_scalar(
                    out=sur(oc), in0=sgr(oc), scalar1=65536.0, scalar2=None,
                    op0=ALU.mult), deps=[tb] + ([t_g2[oc - 2]] if oc >= 2 else []))
                tcf = p.op("v", lambda v, oc=oc: v.tensor_scalar(
                    out=sgr(oc), in0=sur(oc), scalar1=1.0 / 65536.0, scalar2=None,
                    op0=ALU.mult), deps=[tc])
                td = p.op("v", lambda v, oc=oc: v.tensor_tensor(
                    out=big(oc), in0=gir(oc), in1=sgr(oc), op=ALU.mult),
                    deps=[tcf, ta, t_gmx[oc]])
                t_g2[oc] = td
                t_g2m[oc] = p.op("v", lambda v, oc=oc: v.tensor_reduce(
                    out=g2m[:, oc:oc + 1], in_=big(oc), axis=AX.X, op=ALU.max,
                    apply_absolute_value=True), deps=[td])
                war_ev = [t_usp[oc - 1], t_umx[oc - 1]] if oc >= 1 else []
                t_uev[oc] = p.op("s", lambda s, oc=oc, psum=psum: s.activation(
                    out=evr(oc), in_=psum[:, :], func=AF.Copy, scale=RAWS),
                    deps=[last] + war_ev)
                t_umx[oc] = p.op("v", lambda v, oc=oc: v.tensor_reduce(
                    out=umax[:, oc:oc + 1], in_=evr(oc), axis=AX.X, op=ALU.max,
                    apply_absolute_value=True), deps=[t_uev[oc]])
                t_usp[oc] = p.dmaop("y", lambda s, oc=oc: s.dma_start(
                    out=up_sp[oc, :, :], in_=evr(oc)), deps=[t_uev[oc]], ch="us")

            tg2 = cross_max(g2m[:, :], [t_g2m[oc] for oc in range(NO)], 2, 2)
            tup = cross_max(umax[:, :], [t_umx[oc] for oc in range(NO)], 3, 3)
            a2s = p.dmaop("g", lambda g: g.dma_start(out=ar_in[2][0:1, 0:2], in_=red0[0:1, 2:4]),
                          deps=[tg2, tup], ch="gs")
            a2 = p.coll("AllReduce", ALU.max, [ar_in[2][:, 0:2]], [ar_out[2][:, 0:2]], deps=[a2s])
            t_m2u = p.dmaop("g", lambda g: g.dma_start(out=SC[0:1, 6:8], in_=ar_out[2][0:1, 0:2]),
                            deps=[a2], ch="gs")
            q4 = p.op("v", lambda v: v.tensor_tensor(out=SC[0:1, 19:20], in0=SC[0:1, 6:7],
                                                     in1=SC[0:1, 4:5], op=ALU.mult),
                      deps=[t_m2u, t_sg])
            t_s2 = p.op("v", lambda v: v.tensor_scalar(out=SC[0:1, 9:10], in0=SC[0:1, 19:20],
                                                       scalar1=QI, scalar2=1e-12, op0=ALU.mult,
                                                       op1=ALU.max), deps=[q4])
            q5 = p.op("v", lambda v: v.reciprocal(out=SC[0:1, 20:21], in_=SC[0:1, 9:10]), deps=[t_s2])
            t_c3 = p.op("v", lambda v: v.tensor_tensor(out=SC[0:1, 28:29], in0=SC[0:1, 20:21],
                                                       in1=SC[0:1, 4:5], op=ALU.mult), deps=[q5])
            q6 = p.op("v", lambda v: v.tensor_tensor(out=SC[0:1, 21:22], in0=SC[0:1, 7:8],
                                                     in1=SC[0:1, 1:2], op=ALU.mult),
                      deps=[t_m2u, t_sx])
            t_su = p.op("v", lambda v: v.tensor_scalar(out=SC[0:1, 10:11], in0=SC[0:1, 21:22],
                                                       scalar1=QI / RAWS, scalar2=1e-12, op0=ALU.mult,
                                                       op1=ALU.max), deps=[q6])
            q7 = p.op("v", lambda v: v.reciprocal(out=SC[0:1, 22:23], in_=SC[0:1, 10:11]),
                      deps=[t_su])
            q8 = p.op("v", lambda v: v.tensor_tensor(out=SC[0:1, 23:24], in0=SC[0:1, 22:23],
                                                     in1=SC[0:1, 1:2], op=ALU.mult), deps=[q7])
            t_cu = p.op("v", lambda v: v.tensor_scalar(out=SC[0:1, 29:30], in0=SC[0:1, 23:24],
                                                       scalar1=1.0 / RAWS, scalar2=None,
                                                       op0=ALU.mult), deps=[q8])
            bm2 = p.op("t", lambda t: t.matmul(
                ps_mm[0][:, 0:2], lhsT=onesr[:, :], rhs=SC[0:1, 28:30], start=True, stop=True),
                deps=[t_c3, t_cu, t_onesr, t_uev[NO - 2], t_umx[NO - 2]])
            t_bc3 = p.op("s", lambda s: s.activation(out=BC[:, 3:5], in_=ps_mm[0][:, 0:2],
                                                     func=AF.Copy), deps=[bm2])

            if BIS <= 7:
                raise _StopEmission()
            up_mm_done = last
            t_url = {}
            t_prod = {}
            t_pm = {}
            t_ui = {}
            for oc in range(NO):
                war_rl = [t_prod[oc - 2]] if oc >= 2 else [up_mm_done]
                t_url[oc] = p.dmaop("y", lambda s, oc=oc: s.dma_start(
                    out=rlr(oc), in_=up_sp[oc, :, :]), deps=[t_usp[oc]] + war_rl, ch="ur0" if oc % 2 == 0 else "ur1")
                war_gn = [t_prod[oc - 2]] if oc >= 2 else [up_mm_done]
                tgn = p.op("s", lambda s, oc=oc: s.activation(
                    out=gnr(oc), in_=big(oc), func=AF.Copy, scale=BC[:, 3:4]),
                    deps=[t_g2m[oc], t_bc3] + war_gn)
                tui = p.op("s", lambda s, oc=oc: s.activation(
                    out=uir(oc), in_=rlr(oc), func=AF.Copy, scale=BC[:, 4:5]),
                    deps=[t_url[oc], t_bc3] + war_gn)
                t_ui[oc] = tui
                tuf = p.op("v", lambda v, oc=oc: v.tensor_scalar(
                    out=rlr(oc), in0=uir(oc), scalar1=1.0 / 65536.0, scalar2=None,
                    op0=ALU.mult), deps=[tui])
                t_prod[oc] = p.op("v", lambda v, oc=oc: v.tensor_tensor(
                    out=big(oc), in0=gnr(oc), in1=rlr(oc), op=ALU.mult),
                    deps=[tgn, tuf])
                t_pm[oc] = p.op("v", lambda v, oc=oc: v.tensor_reduce(
                    out=pmax[:, oc:oc + 1], in_=big(oc), axis=AX.X, op=ALU.max,
                    apply_absolute_value=True), deps=[t_prod[oc]])

            tpp = cross_max(pmax[:, :], [t_pm[oc] for oc in range(NO)], 4, 4)
            a3s = p.dmaop("g", lambda g: g.dma_start(out=ar_in[3][0:1, 0:1], in_=red0[0:1, 4:5]),
                          deps=[tpp], ch="gs")
            a3 = p.coll("AllReduce", ALU.max, [ar_in[3][:, 0:1]], [ar_out[3][:, 0:1]], deps=[a3s])
            t_mp = p.dmaop("g", lambda g: g.dma_start(out=SC[0:1, 12:13], in_=ar_out[3][0:1, 0:1]),
                           deps=[a3], ch="gs")
            k1 = p.op("v", lambda v: v.tensor_tensor(out=SC[0:1, 24:25], in0=SC[0:1, 9:10],
                                                     in1=SC[0:1, 10:11], op=ALU.mult),
                      deps=[t_s2, t_su])
            k2 = p.op("v", lambda v: v.tensor_tensor(out=SC[0:1, 25:26], in0=SC[0:1, 24:25],
                                                     in1=SC[0:1, 12:13], op=ALU.mult),
                      deps=[k1, t_mp])
            t_s3 = p.op("v", lambda v: v.tensor_scalar(out=SC[0:1, 14:15], in0=SC[0:1, 25:26],
                                                       scalar1=65536.0 * QI, scalar2=1e-12,
                                                       op0=ALU.mult, op1=ALU.max), deps=[k2])
            k3 = p.op("v", lambda v: v.reciprocal(out=SC[0:1, 26:27], in_=SC[0:1, 14:15]),
                      deps=[t_s3])
            k4 = p.op("v", lambda v: v.tensor_tensor(out=SC[0:1, 27:28], in0=SC[0:1, 26:27],
                                                     in1=SC[0:1, 24:25], op=ALU.mult),
                      deps=[k3, k1])
            t_c4 = p.op("v", lambda v: v.tensor_scalar(out=SC[0:1, 13:14], in0=SC[0:1, 27:28],
                                                       scalar1=65536.0, scalar2=None,
                                                       op0=ALU.mult), deps=[k4])
            bm3 = p.op("t", lambda t: t.matmul(
                ps_mm[1][:, 0:2], lhsT=onesr[:, :], rhs=SC[0:1, 13:15], start=True, stop=True),
                deps=[t_c4, t_s3, t_onesr, t_uev[NO - 1], t_umx[NO - 1]])
            t_bc4 = p.op("s", lambda s: s.activation(out=BC[:, 5:7], in_=ps_mm[1][:, 0:2],
                                                     func=AF.Copy), deps=[bm3])

            if BIS <= 8:
                raise _StopEmission()
            t_of = {}
            for oc in range(NO):
                war = [t_of[oc - 2]] if oc >= 2 else [t_prod[NO - 1], t_pm[NO - 1], t_ui[NO - 1]]
                if oc % 2 == 0:
                    ta = p.op("s", lambda s, oc=oc: s.activation(
                        out=oir(oc), in_=big(oc), func=AF.Copy, scale=BC[:, 5:6]),
                        deps=[t_pm[oc], t_bc4] + war)
                else:
                    ta = p.op("v", lambda v, oc=oc: v.tensor_scalar(
                        out=oir(oc), in0=big(oc), scalar1=BC[:, 5:6], scalar2=None,
                        op0=ALU.mult), deps=[t_pm[oc], t_bc4] + war)
                t_of[oc] = p.op("v", lambda v, oc=oc: v.tensor_copy(out=big(oc), in_=oir(oc)),
                                deps=[ta])

            t_wdr = [w_reload(2, c, [up_mm_done]) for c in range(NO)]
            t_x1rl = {}
            t_out = {}
            t_odma = {}
            t_dnev = {}
            lsq4 = {}
            for g in range(NP2):
                war_x1 = [t_out[(g - 2, 0)], t_out[(g - 2, 1)]] if g >= 2 else []
                t_x1rl[g] = p.dmaop("y", lambda s, g=g: s.dma_start(
                    out=x1p(g).rearrange("p (w d) -> p w d", d=D),
                    in_=x1_sp[g * 256:(g + 1) * 256, :].rearrange("(w p) d -> p w d", p=128)),
                    deps=[t_x1sp[g], t_g2[NO - 1], t_usp[NO - 1], t_umx[NO - 1]] + war_x1,
                    ch="xr")
                psd = ps_mm[g % 2]
                for w in range(2):
                    i = 2 * g + w
                    for oc in range(NO):
                        for nh in range(2):
                            dps = []
                            if nh == 0:
                                dps = [t_of[oc], t_wdr[oc]]
                                if oc == 0 and w == 0:
                                    dps += ([t_dnev[(g - 2, 0)], t_dnev[(g - 2, 1)]] if g >= 2
                                            else [t_uev[NO - 2 + g], t_umx[NO - 2 + g]])
                            last = p.op("t", lambda t, oc=oc, nh=nh, i=i, w=w, psd=psd: t.matmul(
                                psd[:, w * D + nh * 512:w * D + (nh + 1) * 512],
                                lhsT=big(oc)[:, i * 128:(i + 1) * 128],
                                rhs=WBUF[:, oc * D + nh * 512:oc * D + (nh + 1) * 512],
                                start=(oc == 0), stop=(oc == NO - 1)), deps=dps + [last])
                m_sl, r_sl, s_sl, q_sl, b_sl = pair_slots(g, 3)
                war_y2 = ([t_t8[(g - 2, 0)], t_t8[(g - 2, 1)], lsq4[(g - 2, 1)],
                           t_odma[(g - 2, 0)], t_odma[(g - 2, 1)]] if g >= 2
                          else [t_prod[NO - 1]])
                tevs = []
                tsqs = []
                for w in range(2):
                    tev = p.op("s", lambda s, g=g, w=w, psd=psd, s_sl=s_sl: s.activation(
                        out=y2p(g)[:, w * D:(w + 1) * D], in_=psd[:, w * D:(w + 1) * D],
                        func=AF.Copy, scale=BC[:, 6:7], accum_out=s_sl[:, w:w + 1]),
                        deps=[last, t_bc4] + war_y2 + tevs)
                    t_dnev[(g, w)] = tev
                    tevs.append(tev)
                    war_sq = [lsq4[(g, 0)]] if w == 1 else ([lsq4[(g - 1, 1)]] if g >= 1
                                                            else last_sq[-1:])
                    tsq = p.op("s", lambda s, g=g, w=w, q_sl=q_sl: s.activation(
                        out=sqs_n, in_=y2p(g)[:, w * D:(w + 1) * D], func=AF.Square,
                        accum_out=q_sl[:, w:w + 1]),
                        deps=[tev, t_g2[NO - 1]] + war_sq)
                    lsq4[(g, w)] = tsq
                    tsqs.append(tsq)
                t3 = p.op("v", lambda v, m_sl=m_sl, s_sl=s_sl: v.tensor_scalar(
                    out=m_sl, in0=s_sl, scalar1=1.0 / D, scalar2=None, op0=ALU.mult),
                    deps=tevs)
                t4 = p.op("v", lambda v, r_sl=r_sl, m_sl=m_sl: v.tensor_tensor(
                    out=r_sl, in0=m_sl, in1=m_sl, op=ALU.mult), deps=[t3])
                t5 = p.op("v", lambda v, r_sl=r_sl, q_sl=q_sl: v.scalar_tensor_tensor(
                    out=r_sl, in0=q_sl, scalar=1.0 / D, in1=r_sl, op0=ALU.mult,
                    op1=ALU.subtract), deps=tsqs + [t4])
                t6 = p.op("s", lambda s, r_sl=r_sl: s.activation(out=r_sl, in_=r_sl, func=AF.Ln,
                                                                 bias=cb[:, 0:1]), deps=[t5, t_cb])
                t7 = p.op("s", lambda s, r_sl=r_sl: s.activation(out=r_sl, in_=r_sl, func=AF.Exp,
                                                                 scale=-0.5), deps=[t6])
                if g == 0:
                    t_t8 = {}
                for w in range(2):
                    war_yn = ([t_out[(g - 2, 0)], t_out[(g - 2, 1)]] if g >= 2
                              else [t_prod[NO - 1]])
                    t_t8[(g, w)] = p.op("v", lambda v, g=g, w=w, m_sl=m_sl, r_sl=r_sl: v.tensor_scalar(
                        out=yn4p(g)[:, w * D:(w + 1) * D], in0=y2p(g)[:, w * D:(w + 1) * D],
                        scalar1=m_sl[:, w:w + 1], scalar2=r_sl[:, w:w + 1],
                        op0=ALU.subtract, op1=ALU.mult),
                        deps=[t7] + war_yn)
                for w in range(2):
                    t_out[(g, w)] = p.op("v", lambda v, g=g, w=w: v.tensor_tensor(
                        out=outp(g)[:, w * D:(w + 1) * D], in0=x1p(g)[:, w * D:(w + 1) * D],
                        in1=yn4p(g)[:, w * D:(w + 1) * D], op=ALU.add),
                        deps=[t_t8[(g, w)], t_x1rl[g], lsq4[(g, w)]])
                    t_odma[(g, w)] = p.dmaop("y", lambda s, g=g, w=w: s.dma_start(
                        out=out_d[g * 256 + w * 128:g * 256 + (w + 1) * 128, :],
                        in_=outp(g)[:, w * D:(w + 1) * D]),
                        deps=[t_out[(g, w)]], ch="ot" if w == 0 else "ot2")

            p.wait("y", t_odma[(NP2 - 1, 0)])
            p.wait("y", t_odma[(NP2 - 1, 1)])
        except _StopEmission:
            tz = p.op("g", lambda g: g.memset(AR32[:, 0:D], 0.0))
            tzs = []
            for i in range(NT):
                tzs.append(p.dmaop("y", lambda s, i=i: s.dma_start(
                    out=out_d[i * 128:(i + 1) * 128, :], in_=AR32[:, 0:D]),
                    deps=[tz], ch="ot"))
            p.wait("y", tzs[-1])

        @block.sync
        def _(eng):
            for f in p.ops["y"]:
                f(eng)

        @block.vector
        def _(eng):
            for f in p.ops["v"]:
                f(eng)

        @block.scalar
        def _(eng):
            for f in p.ops["s"]:
                f(eng)

        @block.gpsimd
        def _(eng):
            for f in p.ops["g"]:
                f(eng)

        @block.tensor
        def _(eng):
            for f in p.ops["t"]:
                f(eng)

    return nc


def kernel(L0_attn_u, L0_attn_log_A_real, L0_attn_B, L0_attn_C, L0_attn_D,
           L0_attn_log_delta, L0_ffn_w_gate, L0_ffn_w_up, L0_ffn_w_down):
    u = np.ascontiguousarray(np.asarray(L0_attn_u, np.float32).reshape(SEQ, D))
    ldr = np.ascontiguousarray(np.asarray(L0_attn_log_delta, np.float32).reshape(8, 128).T)
    dvf = np.asarray(L0_attn_D, np.float32)
    dgm = np.zeros((128, 4096), np.float32)
    for c in range(8):
        dgm[np.arange(128), c * 512 + (c % 4) * 128 + np.arange(128)] = dvf[c * 128:c * 128 + 128]
    dgm = np.ascontiguousarray(dgm)
    lar = np.ascontiguousarray(np.asarray(L0_attn_log_A_real, np.float32))
    btr = np.ascontiguousarray(np.asarray(L0_attn_B, np.float32).T)
    ctr = np.ascontiguousarray(np.asarray(L0_attn_C, np.float32).T)
    wg = np.ascontiguousarray(np.asarray(L0_ffn_w_gate, np.float32))
    wu = np.ascontiguousarray(np.asarray(L0_ffn_w_up, np.float32))
    wd = np.ascontiguousarray(np.asarray(L0_ffn_w_down, np.float32))

    if "nc" not in _CACHED:
        _CACHED["nc"] = build()
    nc = _CACHED["nc"]

    in_maps = []
    for k in range(NCORE):
        cexp = np.array([max(T * (k - 1 - m), 0) for m in range(8)], np.float32)
        cmask = np.array([1.0 if m < k else 0.0 for m in range(8)], np.float32)
        in_maps.append({
            "u_in": u[k * T:(k + 1) * T],
            "ldr": ldr, "dgm": dgm, "lar": lar, "btr": btr, "ctr": ctr,
            "wg_nat": wg[k * OG:(k + 1) * OG], "wu_nat": wu[k * OG:(k + 1) * OG],
            "wd_nat": wd[k * HG:(k + 1) * HG],
            "cexp": cexp, "cmask": cmask,
            "idm": np.eye(128, dtype=np.float16),
        })
    res = run_bass_kernel_spmd(nc, in_maps, core_ids=list(range(NCORE)))
    out = np.concatenate([np.asarray(r["out_d"], np.float32) for r in res.results], axis=0)
    return out.reshape(1, SEQ, D)


# version 62
# speedup vs baseline: 1.0677x; 1.0677x over previous
import contextlib
import os
import numpy as np
import concourse.bass as bass
import concourse.mybir as mybir
from concourse.bass_utils import run_bass_kernel_spmd

F32 = mybir.dt.float32
F16 = mybir.dt.float16
I16 = mybir.dt.int16
U16 = mybir.dt.uint16
I8 = mybir.dt.int8
AF = mybir.ActivationFunctionType
ALU = mybir.AluOpType
AX = mybir.AxisListType

NCORE = 8
SEQ = 16384
T = SEQ // NCORE
D = 1024
NS = 64
H = 3072
NT = T // 128
ND = D // 128
NO = H // 128
OG = 384
HG = 128
EPS = 1e-5
QI = 1.0 / 32767.0
RAWS = 0.125

_CACHED = {}


class Prog:
    def __init__(self):
        self.ops = {e: [] for e in "yvsgt"}
        self.cnt = {e: 0 for e in "yvsgt"}
        self.dma = {}
        self.maxw = {e: {} for e in "yvsgt"}
        self.ES = {}
        self.DS = {}
        self.CS = None
        self.ncoll = 0

    def wait(self, e, tok):
        kind = tok[0]
        if e == "t" and kind == "c" and tok[1] == "t":
            return
        if kind == "k":
            v = tok[1]
            key = ("k", "k")
            sem = self.CS
        else:
            _, src, v = tok
            key = (kind, src)
            sem = self.ES[src] if kind == "c" else self.DS[src]
        if self.maxw[e].get(key, 0) >= v:
            return
        self.maxw[e][key] = v
        self.ops[e].append(lambda eng, sem=sem, v=v: eng.wait_ge(sem, v))

    def op(self, e, f, deps=()):
        for t in deps:
            self.wait(e, t)
        self.cnt[e] += 1
        v = self.cnt[e]

        def run(eng, f=f, e=e):
            f(eng).then_inc(self.ES[e], 1)
        self.ops[e].append(run)
        return ("c", e, v)

    def dmaop(self, q, f, deps=(), ch=None):
        assert ch is not None
        cur = self.dma.get(ch, 0)
        if cur > 0:
            self.wait(q, ("d", ch, cur))
        for t in deps:
            self.wait(q, t)
        self.dma[ch] = cur + 16
        v = self.dma[ch]

        def run(eng, f=f, ch=ch):
            f(eng).then_inc(self.DS[ch], 16)
        self.ops[q].append(run)
        return ("d", ch, v)

    def coll(self, kind, alu, ins, outs, deps=()):
        for t in deps:
            self.wait("g", t)
        self.ncoll += 1
        v = self.ncoll

        def run(eng, kind=kind, alu=alu, ins=ins, outs=outs):
            eng.collective_compute(kind, alu, replica_groups=[list(range(NCORE))],
                                   ins=ins, outs=outs).then_inc(self.CS, 1)
        self.ops["g"].append(run)
        return ("k", v)


class _StopEmission(Exception):
    pass


def build():
    BIS = float(os.environ.get("BISECT", "9"))
    nc = bass.Bass("TRN2", target_bir_lowering=False)

    u_in = nc.dram_tensor("u_in", [T, D], F32, kind="ExternalInput").ap()
    ldr = nc.dram_tensor("ldr", [128, 8], F32, kind="ExternalInput").ap()
    dgm = nc.dram_tensor("dgm", [128, 4096], F32, kind="ExternalInput").ap()
    lar = nc.dram_tensor("lar", [NS], F32, kind="ExternalInput").ap()
    btr = nc.dram_tensor("btr", [D, NS], F32, kind="ExternalInput").ap()
    ctr = nc.dram_tensor("ctr", [NS, D], F32, kind="ExternalInput").ap()
    wg_nat = nc.dram_tensor("wg_nat", [OG, D], F32, kind="ExternalInput").ap()
    wu_nat = nc.dram_tensor("wu_nat", [OG, D], F32, kind="ExternalInput").ap()
    wd_nat = nc.dram_tensor("wd_nat", [HG, H], F32, kind="ExternalInput").ap()
    cexp = nc.dram_tensor("cexp", [8], F32, kind="ExternalInput").ap()
    cmask = nc.dram_tensor("cmask", [8], F32, kind="ExternalInput").ap()
    idm = nc.dram_tensor("idm", [128, 128], F16, kind="ExternalInput").ap()
    out_d = nc.dram_tensor("out_d", [T, D], F32, kind="ExternalOutput").ap()

    SHARD = OG * D * 2 + HG * H
    cc_in = nc.dram_tensor("cc_in", [1, SHARD // 2], F32).ap()
    cc_out = nc.dram_tensor("cc_out", [NCORE, OG * D // 2], F32, addr_space="Shared").ap()
    cc_out2 = nc.dram_tensor("cc_out2", [NCORE, (SHARD - OG * D) // 2], F32, addr_space="Shared").ap()
    ags_in = nc.dram_tensor("ags_in", [1, NS], F32).ap()
    ags_out = nc.dram_tensor("ags_out", [NCORE, NS], F32, addr_space="Shared").ap()
    ar_in = [nc.dram_tensor(f"ar{i}_in", [1, 2], F32).ap() for i in range(4)]
    ar_out = [nc.dram_tensor(f"ar{i}_out", [1, 2], F32, addr_space="Shared").ap() for i in range(4)]
    bb = [nc.dram_tensor(f"bb{i}", [1, 4], F32).ap() for i in range(4)]
    x1_sp = nc.dram_tensor("x1_sp", [T, D], F32).ap()
    up_sp = nc.dram_tensor("up_sp", [NO, 128, T], F16).ap()
    sv_b = nc.dram_tensor("sv_b", [1, 128 * 32 * 2], F32).ap()
    a_b = nc.dram_tensor("a_b", [1, NS], F32).ap()

    cc_in16 = cc_in.bitcast(F16)
    cc_out16 = cc_out.bitcast(F16)
    cc_out216 = cc_out2.bitcast(F16)

    p = Prog()

    with contextlib.ExitStack() as st:
        E = st.enter_context

        ARENA = E(nc.sbuf_tensor("ARENA", [128, NO * T], F16))
        AR32 = ARENA[:, :].bitcast(F32)
        XT = E(nc.sbuf_tensor("XT", [128, ND * T], F16, is_transpose_dest=True))
        XT32 = XT[:, :].bitcast(F32)
        XTI = XT[:, :].bitcast(I16)
        WBUF = E(nc.sbuf_tensor("WBUF", [128, ND * H], F16))
        WB32 = WBUF[:, :].bitcast(F32)
        WB8 = WBUF[:, :].bitcast(I8)
        RB = E(nc.sbuf_tensor("RB", [128, 10240], F16))
        RB32 = RB[:, :].bitcast(F32)
        RBI = RB[:, :].bitcast(I16)
        RBU = RB[:, :].bitcast(U16)

        def upr(k):
            s = k % 4
            if s == 3:
                return AR32[:, 8192:10240]
            return AR32[:, s * 2048:(s + 1) * 2048]

        dgq = ARENA[:, 12288:16384]

        def x1t(g):
            return AR32[:, 10240 + (g % 2) * 2048:10240 + (g % 2 + 1) * 2048]
        BU = AR32[0:NS, 14336:14336 + T]
        XS0 = AR32[0:NS, 14336 + T:14336 + 2 * T]
        ABC = AR32[0:NS, 14336 + 2 * T:14336 + 3 * T]
        IOT = AR32[0:NS, 14336 + 3 * T:14336 + 4 * T]
        def yri(g):
            return AR32[:, 14336 + (g % 2) * 2048:14336 + (g % 2 + 1) * 2048]

        def ynri(g):
            return AR32[:, 18432 + (g % 2) * 2048:18432 + (g % 2 + 1) * 2048]
        sqs_a = AR32[:, 22528:23552]
        xsf = ARENA[0:NS, 47104:47104 + T]

        def big(oc):
            return ARENA[:, oc * T:(oc + 1) * T]

        def rlr(oc):
            return XT[:, (oc % 2) * T:(oc % 2 + 1) * T]

        def gnr(oc):
            return XTI[:, (2 + oc % 2) * T:(3 + oc % 2) * T]

        def oir(oc):
            return XTI[:, (oc % 2) * T:(oc % 2 + 1) * T]

        def uir(oc):
            return XTI[:, (6 + oc % 2) * T:(7 + oc % 2) * T]

        def y2r(i):
            return XT32[:, 4 * T + (i % 2) * 512:4 * T + (i % 2 + 1) * 512]
        def y2p(g):
            return XT32[:, (g % 2) * T:(g % 2 + 1) * T]

        def outp(g):
            return y2p(g)

        def yn4p(g):
            return XT32[:, 2 * T + (g % 2) * T:2 * T + (g % 2 + 1) * T]
        sqs_n = RB32[:, 0:D]

        def x1p(g):
            return RB32[:, 1024 + (g % 2) * T:1024 + (g % 2 + 1) * T]

        def xn1r(g):
            return RB[:, (g % 2) * T:(g % 2 + 1) * T]

        def xn3r(g):
            return RB[:, (2 + g % 2) * T:(3 + g % 2) * T]
        def gir(oc):
            if oc % 2 == 0:
                return RBI[:, 0:T]
            return RBI[:, 8192:8192 + T]

        def sgr(oc):
            return RB[:, T + (oc % 2) * T:T + (oc % 2 + 1) * T]

        def sur(oc):
            return RBU[:, T + (oc % 2) * T:T + (oc % 2 + 1) * T]

        def evr(oc):
            return RB[:, 3 * T:4 * T]

        def natr(j):
            return WB32[:, (j % 2) * D:(j % 2 + 1) * D]
        def dqtmp(j):
            return WB32[:, (2 + j % 2) * 1024:(3 + j % 2) * 1024]
        def wq8(j):
            return WB8[:, (16 + j % 2) * 1024:(17 + j % 2) * 1024]
        def wdq(j):
            return WBUF[:, (9 + j % 2) * 1024:(10 + j % 2) * 1024]
        def wsht(j):
            return WBUF[:, (11 + j % 2) * 1024:(12 + j % 2) * 1024]
        def sqd(j):
            return sqd_[:, (j % 2) * 32:(j % 2 + 1) * 32]
        def sqi(j):
            return sqi_[:, (j % 2) * 32:(j % 2 + 1) * 32]

        stm = E(nc.sbuf_tensor("stm", [128, 20 * NT], F32))
        gmax = E(nc.sbuf_tensor("gmax", [128, NO], F32))
        umax = E(nc.sbuf_tensor("umax", [128, NO], F32))
        g2m = E(nc.sbuf_tensor("g2m", [128, NO], F32))
        pmax = E(nc.sbuf_tensor("pmax", [128, NO], F32))
        xmax = E(nc.sbuf_tensor("xmax", [128, NT], F32))
        red1 = E(nc.sbuf_tensor("red1", [128, 8], F32))
        red0 = E(nc.sbuf_tensor("red0", [1, 8], F32))
        SC = E(nc.sbuf_tensor("SC", [1, 32], F32))
        BC = E(nc.sbuf_tensor("BC", [128, 8], F32))

        ldt = E(nc.sbuf_tensor("ldt", [128, 8], F32))
        dlt = E(nc.sbuf_tensor("dlt", [128, 8], F32))
        dsc = E(nc.sbuf_tensor("dsc", [128, 16], F32))
        arow = E(nc.sbuf_tensor("arow", [1, NS], F32))
        a64 = E(nc.sbuf_tensor("a64", [NS, 1], F32))
        lnac = E(nc.sbuf_tensor("lnac", [NS, 1], F32))
        carry = E(nc.sbuf_tensor("carry", [NS, 1], F32))
        bt16 = E(nc.sbuf_tensor("bt16", [128, ND * NS], F16))
        dbt = E(nc.sbuf_tensor("dbt", [128, ND * NS], F16))
        ct16 = E(nc.sbuf_tensor("ct16", [NS, D], F16))
        arb = E(nc.sbuf_tensor("arb", [128, NS], F32))
        abar = E(nc.sbuf_tensor("abar", [128, NS], F32))
        abar2 = E(nc.sbuf_tensor("abar2", [128, NS], F32))
        ones1 = E(nc.sbuf_tensor("ones1", [128, 1], F32))
        onesr = E(nc.sbuf_tensor("onesr", [1, 128], F32))
        cb = E(nc.sbuf_tensor("cb", [128, 2], F32))
        smT = E(nc.sbuf_tensor("smT", [NS, 8], F32))
        ebx = E(nc.sbuf_tensor("ebx", [NS, 8], F32))
        mbx = E(nc.sbuf_tensor("mbx", [NS, 8], F32))
        cw1 = E(nc.sbuf_tensor("cw1", [NS, 8], F32))
        cw2 = E(nc.sbuf_tensor("cw2", [NS, 8], F32))
        sqd_ = E(nc.sbuf_tensor("sqd", [128, 64], F32))
        sqi_ = E(nc.sbuf_tensor("sqi", [128, 64], F32))
        idn = E(nc.sbuf_tensor("idn", [128, 128], F16))

        ps_pre = contextlib.ExitStack()
        ps_bu = ps_pre.enter_context(nc.psum_tensor("ps_bu", [NS, 1024], F32))
        ps_y = ps_pre.enter_context(nc.psum_tensor("ps_y", [128, T], F32))
        ps_tr = [ps_pre.enter_context(nc.psum_tensor(f"ps_tr{k}", [128, 1024], F16)) for k in range(2)]

        for e in "yvsgt":
            p.ES[e] = E(nc.semaphore(f"ES_{e}"))
        CHANNELS = (["ua", "ub", "uc", "ud", "wl", "cs", "cs2", "sp", "us", "ur0", "ur1", "xr", "ot", "ot2", "tr0", "tr1",
                     "gs", "gx", "q0", "q1"] + [f"t{c}" for c in range(8)])
        for ch in CHANNELS:
            p.DS[ch] = E(nc.semaphore(f"DS_{ch}"))
        p.CS = E(nc.semaphore("CS"))

        block = E(nc.Block())

        try:
            t_idn = p.dmaop("y", lambda s: s.dma_start(out=idn[:, :], in_=idm[:, :]), ch="wl")
            _tr_state = {0: None, 1: None, "n": 0}

            def pe_transpose8(src_1024, dst_view, deps, ev_eng="s"):
                k = _tr_state["n"] % 2
                _tr_state["n"] += 1
                war = [_tr_state[k]] if _tr_state[k] is not None else []
                tt = None
                for cc in range(8):
                    tt = p.op("t", lambda t, src_1024=src_1024, k=k, cc=cc: t.transpose(
                        ps_tr[k][:, cc * 128:(cc + 1) * 128],
                        src_1024[:, cc * 128:(cc + 1) * 128], idn[:, :]),
                        deps=(list(deps) + war + [t_idn]) if cc == 0 else [tt])
                src_ps = ps_tr[k][:, :].rearrange("p (c q) -> p c q", q=128)
                if ev_eng == "s":
                    ev = p.op("s", lambda s, dst_view=dst_view, src_ps=src_ps: s.copy(
                        dst_view, src_ps), deps=[tt])
                else:
                    ev = p.op("v", lambda v, dst_view=dst_view, src_ps=src_ps: v.tensor_copy(
                        out=dst_view, in_=src_ps), deps=[tt])
                _tr_state[k] = ev
                return ev
            t_ldr = p.dmaop("y", lambda s: s.dma_start(out=ldt[:, :], in_=ldr[:, :]), ch="wl")
            t_bt = p.dmaop("g", lambda g: g.dma_start(
                out=bt16[:, :].rearrange("p (c n) -> p c n", n=NS),
                in_=btr.rearrange("(c p) n -> p c n", p=128)), ch="gx")
            t_ct = p.dmaop("g", lambda g: g.dma_start(out=ct16[:, :], in_=ctr[:, :]), ch="gx")
            t_dgq = p.dmaop("g", lambda g: g.dma_start(out=dgq[:, :], in_=dgm[:, :]), ch="gs")
            t_lar = p.dmaop("g", lambda g: g.dma_start(
                out=arb[:, :], in_=lar.rearrange("(o n) -> o n", o=1).broadcast_to([128, NS])), ch="gx")
            t_ebx = p.dmaop("g", lambda g: g.dma_start(
                out=ebx[:, :], in_=cexp.rearrange("(o m) -> o m", o=1).broadcast_to([NS, 8])), ch="gx")
            t_mbx = p.dmaop("g", lambda g: g.dma_start(
                out=mbx[:, :], in_=cmask.rearrange("(o m) -> o m", o=1).broadcast_to([NS, 8])), ch="gs")
            t_ones = p.op("g", lambda g: g.memset(ones1[:, :], 1.0))
            t_onesr = p.op("g", lambda g: g.memset(onesr[:, :], 1.0))
            t_cb0 = p.op("g", lambda g: g.memset(cb[:, 0:1], EPS))
            t_cb = p.op("g", lambda g: g.memset(cb[:, 1:2], 1.0), deps=[t_cb0])
            t_iota = p.op("g", lambda g: g.iota(IOT, pattern=[[1, T]], base=1,
                                                channel_multiplier=0,
                                                allow_small_or_imprecise_dtypes=True))

            d_abs, d_en = dsc[:, 0:8], dsc[:, 8:16]
            b1 = p.op("s", lambda s: s.activation(out=d_abs, in_=ldt[:, :], func=AF.Abs), deps=[t_ldr])
            b2 = p.op("s", lambda s: s.activation(out=d_en, in_=d_abs, func=AF.Exp, scale=-1.0), deps=[b1])
            b3 = p.op("s", lambda s: s.activation(out=d_en, in_=d_en, func=AF.Ln, bias=cb[:, 1:2]), deps=[b2, t_cb])
            b4 = p.op("v", lambda v: v.tensor_scalar(out=dlt[:, :], in0=ldt[:, :], scalar1=0.0,
                                                     scalar2=None, op0=ALU.max), deps=[t_ldr])
            b5 = p.op("v", lambda v: v.tensor_tensor(out=dlt[:, :], in0=dlt[:, :], in1=d_en,
                                                     op=ALU.add), deps=[b3, b4])
            b6 = p.op("s", lambda s: s.activation(out=arb[:, :], in_=arb[:, :], func=AF.Exp),
                      deps=[t_lar])
            b7 = p.op("v", lambda v: v.tensor_scalar(out=arb[:, :], in0=arb[:, :], scalar1=-1.0,
                                                     scalar2=None, op0=ALU.mult), deps=[b6])
            t_dbt = [p.op("v", lambda v, c=c: v.tensor_scalar(
                out=dbt[:, c * NS:(c + 1) * NS], in0=bt16[:, c * NS:(c + 1) * NS],
                scalar1=dlt[:, c:c + 1], scalar2=None, op0=ALU.mult), deps=[t_bt, b5])
                for c in range(ND)]
            ps_a = ps_bu[0:1, 0:NS]
            t_am = []
            for c in range(ND):
                te = p.op("v", lambda v, c=c: v.tensor_scalar(
                    out=abar[:, :], in0=arb[:, :], scalar1=dlt[:, c:c + 1], scalar2=None,
                    op0=ALU.mult), deps=[b7, b5] + t_am[-1:])
                tx = p.op("s", lambda s: s.activation(out=abar2[:, :], in_=abar[:, :], func=AF.Exp),
                          deps=[te])
                t_am.append(p.op("t", lambda t, c=c: t.matmul(
                    ps_a, lhsT=ones1[:, :], rhs=abar2[:, :], start=(c == 0), stop=(c == ND - 1)),
                    deps=[tx, t_ones]))
            t_arw = p.op("v", lambda v: v.tensor_scalar(out=arow[:, :], in0=ps_a, scalar1=1.0 / D,
                                                        scalar2=None, op0=ALU.mult), deps=[t_am[-1]])
            ta1 = p.dmaop("g", lambda g: g.dma_start(out=a_b[0:1, :], in_=arow[:, :]), deps=[t_arw], ch="gs")
            ta2 = p.dmaop("g", lambda g: g.dma_start(
                out=a64[:, :], in_=a_b[0:1, :].rearrange("o (n x) -> (o n) x", x=1),
                allow_slow_non_contiguous=True), deps=[ta1], ch="gs")
            t_ln = p.op("s", lambda s: s.activation(out=lnac[:, :], in_=a64[:, :], func=AF.Ln),
                        deps=[ta2])
            t_lnc = p.op("v", lambda v: v.tensor_scalar(out=lnac[:, :], in0=lnac[:, :],
                                                        scalar1=-1e30, scalar2=None, op0=ALU.max),
                         deps=[t_ln])

            if BIS <= 1:
                raise _StopEmission()
            tiles = []
            for k in range(3):
                tiles.append(("g", wg_nat[k * 128:(k + 1) * 128, :], 0, OG, k * 128))
            for k in range(3):
                tiles.append(("u", wu_nat[k * 128:(k + 1) * 128, :], OG * D, OG, k * 128))
            for k in range(3):
                tiles.append(("d", wd_nat[:, k * D:(k + 1) * D], 2 * OG * D + k * D * HG, HG, 0))

            dq_store = []
            dq_last = []
            DQ = {}

            def dq_tile(j):
                wname, src_, base, ostr, coff = tiles[j]
                pv = DQ.get(j - 2, {})
                pw = DQ.get(j - 1, {})
                cur = {}
                tl = p.dmaop("y", lambda s, j=j, src_=src_: s.dma_start(out=natr(j), in_=src_),
                             deps=([pv["t4a"]] if pv else []), ch="wl")
                t1 = p.op("v", lambda v, j=j: v.tensor_reduce(
                    out=sqd(j), in_=natr(j).rearrange("p (b k) -> p b k", k=32),
                    axis=AX.X, op=ALU.max, apply_absolute_value=True),
                    deps=[tl] + ([pv["t5"]] if pv else []))
                t2 = p.op("v", lambda v, j=j: v.tensor_scalar(
                    out=sqd(j), in0=sqd(j), scalar1=1.0 / 7.0, scalar2=1e-12,
                    op0=ALU.mult, op1=ALU.max), deps=[t1])
                t3 = p.op("v", lambda v, j=j: v.reciprocal(out=sqi(j), in_=sqd(j)),
                          deps=[t2] + ([pv["t4a"]] if pv else []))
                t4a = p.op("g", lambda g, j=j: g.tensor_tensor(
                    out=dqtmp(j).rearrange("p (b k) -> p b k", k=32),
                    in0=natr(j).rearrange("p (b k) -> p b k", k=32),
                    in1=sqi(j).unsqueeze(2).broadcast_to([128, 32, 32]), op=ALU.mult),
                    deps=[t3, tl] + ([pv["t4"]] if pv else []))
                cur["t4a"] = t4a
                t4 = p.op("v", lambda v, j=j: v.tensor_copy(out=wq8(j), in_=dqtmp(j)),
                          deps=[t4a] + ([pv["t5"]] if pv else []))
                cur["t4"] = t4
                t5 = p.op("v", lambda v, j=j: v.tensor_tensor(
                    out=wdq(j).rearrange("p (b k) -> p b k", k=32),
                    in0=wq8(j).rearrange("p (b k) -> p b k", k=32),
                    in1=sqd(j).unsqueeze(2).broadcast_to([128, 32, 32]), op=ALU.mult),
                    deps=[t4] + ([pv["tr"]] if pv else []))
                cur["t5"] = t5
                tr = pe_transpose8(wdq(j), wsht(j).rearrange("p (c q) -> p c q", q=128),
                                   deps=[t5] + ([pv["st"]] if pv else []), ev_eng="v")
                cur["tr"] = tr
                st_tok = p.dmaop("g", lambda g, j=j, base=base, ostr=ostr, coff=coff: g.dma_start(
                    out=bass.AP(cc_in16.tensor, base + coff,
                                [[ostr, 128], [128 * ostr, 8], [1, 128]]),
                    in_=wsht(j).rearrange("p (c q) -> p c q", q=128)), deps=[tr],
                    ch="cs" if j % 2 == 0 else "cs2")
                cur["st"] = st_tok
                dq_store.append(st_tok)
                dq_last.extend([t5, tr, st_tok])
                DQ[j] = cur

            _agw = [None]
            t_agw2 = [None]

            def emit_gate_ag():
                _agw[0] = p.coll("AllGather", ALU.bypass, [cc_in[:, 0:OG * D // 2]],
                                 [cc_out[:, :]], deps=dq_store[0:3])

            if BIS <= 2:
                raise _StopEmission()
            def w_reload(w, c, deps):
                if w == 0:
                    src = cc_out16[:, 0:OG * D].rearrange("r (d o) -> d r o", o=OG)
                    src = src[c * 128:(c + 1) * 128, :, :]
                    dst = WBUF[:, c * H:(c + 1) * H].rearrange("p (r o) -> p r o", o=OG)
                elif w == 1:
                    src = cc_out216[:, 0:OG * D].rearrange("r (d o) -> d r o", o=OG)
                    src = src[c * 128:(c + 1) * 128, :, :]
                    dst = WBUF[:, c * H:(c + 1) * H].rearrange("p (r o) -> p r o", o=OG)
                else:
                    base = OG * D
                    src = cc_out216[:, base:base + HG * H].rearrange("r (q h) -> q r h", h=HG)
                    src = src[c * 128:(c + 1) * 128, :, :]
                    dst = WBUF[:, c * D:(c + 1) * D].rearrange("p (r h) -> p r h", h=HG)
                tok = p.dmaop("y", lambda s, src=src, dst=dst: s.dma_start(out=dst, in_=src),
                              deps=deps, ch="wl")
                return tok

            def ln_stats(i, src_tile, grp, deps, sq_scratch, war_sq):
                base = grp * 4 * NT
                m_sl = stm[:, base + i:base + i + 1]
                r_sl = stm[:, base + NT + i:base + NT + i + 1]
                s_sl = stm[:, base + 2 * NT + i:base + 2 * NT + i + 1]
                q_sl = stm[:, base + 3 * NT + i:base + 3 * NT + i + 1]
                t1 = p.op("v", lambda v: v.tensor_reduce(out=s_sl, in_=src_tile, axis=AX.X,
                                                         op=ALU.add), deps=deps)
                t2 = p.op("s", lambda s: s.activation(out=sq_scratch, in_=src_tile, func=AF.Square,
                                                      accum_out=q_sl), deps=deps + war_sq)
                t3 = p.op("v", lambda v: v.tensor_scalar(out=m_sl, in0=s_sl, scalar1=1.0 / D,
                                                         scalar2=None, op0=ALU.mult), deps=[t1])
                t4 = p.op("v", lambda v: v.tensor_scalar(out=r_sl, in0=m_sl, scalar1=m_sl,
                                                         scalar2=None, op0=ALU.mult), deps=[t3])
                t5 = p.op("v", lambda v: v.scalar_tensor_tensor(out=r_sl, in0=q_sl, scalar=1.0 / D,
                                                                in1=r_sl, op0=ALU.mult,
                                                                op1=ALU.subtract), deps=[t2, t4])
                t6 = p.op("s", lambda s: s.activation(out=r_sl, in_=r_sl, func=AF.Ln, bias=cb[:, 0:1]),
                          deps=[t5, t_cb])
                t7 = p.op("s", lambda s: s.activation(out=r_sl, in_=r_sl, func=AF.Exp, scale=-0.5),
                          deps=[t6])
                return t7, m_sl, r_sl, t2

            def pair_slots(g, grp):
                base = grp * 5 * NT
                i0 = 2 * g
                return (stm[:, base + i0:base + i0 + 2],
                        stm[:, base + NT + i0:base + NT + i0 + 2],
                        stm[:, base + 2 * NT + i0:base + 2 * NT + i0 + 2],
                        stm[:, base + 3 * NT + i0:base + 3 * NT + i0 + 2],
                        stm[:, base + 4 * NT + i0:base + 4 * NT + i0 + 2])

            sqch = [None]

            def ln_pair(g, src_pair, grp, deps, war_sq):
                m_sl, r_sl, s_sl, q_sl, b_sl = pair_slots(g, grp)
                t1 = p.op("v", lambda v: v.tensor_reduce(
                    out=s_sl, in_=src_pair.rearrange("p (w d) -> p w d", d=D),
                    axis=AX.X, op=ALU.add), deps=deps)
                tq0 = p.op("s", lambda s: s.activation(out=sqs_a, in_=src_pair[:, 0:D],
                                                       func=AF.Square, accum_out=q_sl[:, 0:1]),
                           deps=deps + ([sqch[0]] if sqch[0] is not None else []))
                tq1 = p.op("s", lambda s: s.activation(out=sqs_a, in_=src_pair[:, D:2 * D],
                                                       func=AF.Square, accum_out=q_sl[:, 1:2]),
                           deps=deps + [tq0])
                sqch[0] = tq1
                t3 = p.op("v", lambda v: v.tensor_scalar(out=m_sl, in0=s_sl, scalar1=1.0 / D,
                                                         scalar2=None, op0=ALU.mult), deps=[t1])
                t4 = p.op("v", lambda v: v.tensor_tensor(out=r_sl, in0=m_sl, in1=m_sl,
                                                         op=ALU.mult), deps=[t3])
                t5 = p.op("v", lambda v: v.scalar_tensor_tensor(out=r_sl, in0=q_sl, scalar=1.0 / D,
                                                                in1=r_sl, op0=ALU.mult,
                                                                op1=ALU.subtract), deps=[tq1, t4])
                t6 = p.op("s", lambda s: s.activation(out=r_sl, in_=r_sl, func=AF.Ln,
                                                      bias=cb[:, 0:1]), deps=[t5, t_cb])
                t7 = p.op("s", lambda s: s.activation(out=r_sl, in_=r_sl, func=AF.Exp,
                                                      scale=-0.5), deps=[t6])
                t8 = p.op("v", lambda v: v.scalar_tensor_tensor(out=b_sl, in0=m_sl, scalar=-1.0,
                                                                in1=r_sl, op0=ALU.mult,
                                                                op1=ALU.mult), deps=[t7])
                return t8, m_sl, r_sl, b_sl

            def cross_max(src_ap, deps, r1slot, r0slot):
                ta = p.op("v", lambda v: v.tensor_reduce(out=red1[:, r1slot:r1slot + 1], in_=src_ap,
                                                         axis=AX.X, op=ALU.max), deps=deps)
                tb = p.op("g", lambda g: g.tensor_reduce(out=red0[0:1, r0slot:r0slot + 1],
                                                         in_=red1[:, r1slot:r1slot + 1],
                                                         axis=AX.C, op=ALU.max), deps=[ta])
                return tb

            NP2 = NT // 2
            t_tr1 = [[None] * ND for _ in range(NT)]
            t_xn1 = {}
            last_sq = []
            lst1 = {}
            t_up = {}
            UCH = ["ua", "ub", "uc", "ud"]
            for g in range(3):
                h0 = p.dmaop("y", lambda s, g=g: s.dma_start(
                    out=upr(g).rearrange("p (w d) -> p w d", d=D)[:, 0:1, :],
                    in_=u_in[g * 256:g * 256 + 128, :].rearrange("(o p) d -> p o d", p=128)),
                    deps=[], ch=UCH[(2 * g) % 4])
                h1 = p.dmaop("y", lambda s, g=g: s.dma_start(
                    out=upr(g).rearrange("p (w d) -> p w d", d=D)[:, 1:2, :],
                    in_=u_in[g * 256 + 128:(g + 1) * 256, :].rearrange("(o p) d -> p o d", p=128)),
                    deps=[], ch=UCH[(2 * g + 1) % 4])
                t_up[g] = (h0, h1)
            def emitDA(g):
                ud = list(t_up[g]) if isinstance(t_up[g][0], tuple) else [t_up[g]]
                tst = ln_pair(g, upr(g), 0, ud, last_sq[-1:])[0]
                last_sq.append(tst)
                lst1[g] = tst

            def emitDB(g):
                tst = lst1[g]
                for w in range(2):
                    i = 2 * g + w
                    war = [t_tr1[i - 4][0] for _ in range(1)] if i >= 4 else []
                    tap = p.op("s", lambda s, g=g, w=w, i=i: s.activation(
                        out=xn1r(g)[:, w * D:(w + 1) * D], in_=upr(g)[:, w * D:(w + 1) * D],
                        func=AF.Identity, scale=pair_slots(g, 0)[1][:, w:w + 1],
                        bias=pair_slots(g, 0)[4][:, w:w + 1]), deps=[tst] + war)
                    t_xn1[i] = tap
                    dstv = XT[:, :].rearrange("p (c t) -> p c t", t=T)[:, :, i * 128:(i + 1) * 128]
                    ev1 = pe_transpose8(xn1r(g)[:, w * D:(w + 1) * D], dstv, deps=[tap], ev_eng="s")
                    for c in range(ND):
                        t_tr1[i][c] = ev1
                if g + 3 < NP2:
                    dd = [t_xn1[2 * g - 1], lst1[g - 1]] if g >= 1 else []
                    t_up[g + 3] = p.dmaop("y", lambda s, g=g: s.dma_start(
                        out=upr(g + 3).rearrange("p (w d) -> p w d", d=D),
                        in_=u_in[(g + 3) * 256:(g + 4) * 256, :].rearrange("(w p) d -> p w d", p=128)),
                        deps=dd, ch=UCH[(g + 3) % 4])
                if g < 8:
                    dq_tile(g)
                if g == 2:
                    emit_gate_ag()

            emitDA(0)
            for g in range(1, NP2):
                emitDA(g)
                emitDB(g - 1)
            emitDB(NP2 - 1)
            dq_tile(8)

            t_bu_mm = []
            t_bu_ev = []
            for h in range(2):
                dps_ev = [t_bu_ev[-1]] if t_bu_ev else []
                for c in range(ND):
                    for tq2 in range(2):
                        tq = 2 * h + tq2
                        dps = [t_tr1[4 * tq + j][c] for j in range(4)] + [t_dbt[c], t_arw] + dps_ev
                        t_bu_mm.append(p.op("t", lambda t, c=c, tq=tq, tq2=tq2: t.matmul(
                            ps_bu[:, tq2 * 512:(tq2 + 1) * 512],
                            lhsT=dbt[:, c * NS:(c + 1) * NS],
                            rhs=XT[:, c * T + tq * 512:c * T + (tq + 1) * 512],
                            start=(c == 0), stop=(c == ND - 1)), deps=dps))
                t_bu_ev.append(p.op("v", lambda v, h=h: v.tensor_copy(
                    out=BU[:, h * 1024:(h + 1) * 1024], in_=ps_bu[:, :]), deps=[t_bu_mm[-1]]))
            t_bu = t_bu_ev[-1]

            t_abc = p.op("v", lambda v: v.tensor_scalar(out=ABC, in0=IOT, scalar1=0.0,
                                                        scalar2=a64[:, 0:1], op0=ALU.mult,
                                                        op1=ALU.add), deps=[t_iota, ta2])
            t_scan = p.op("v", lambda v: v.tensor_tensor_scan(out=XS0, data0=ABC, data1=BU,
                                                              initial=0.0, op0=ALU.mult,
                                                              op1=ALU.add), deps=[t_abc, t_bu])
            t_sks = p.dmaop("g", lambda g: g.dma_start(out=ags_in[0:1, :], in_=XS0[:, T - 1:T],
                                                       allow_slow_non_contiguous=True),
                            deps=[t_scan], ch="gs")
            t_agS = p.coll("AllGather", ALU.bypass, [ags_in[:, :]], [ags_out[:, :]], deps=[t_sks])
            t_agw2[0] = p.coll("AllGather", ALU.bypass,
                               [cc_in[:, OG * D // 2:]], [cc_out2[:, :]], deps=dq_store)
            t_sm = p.dmaop("g", lambda g: g.dma_start(out=smT[:, :],
                                                      in_=ags_out.rearrange("m n -> n m"),
                                                      allow_slow_non_contiguous=True),
                           deps=[t_agS], ch="gs")
            c1_ = p.op("v", lambda v: v.tensor_scalar(out=cw1[:, :], in0=ebx[:, :],
                                                      scalar1=lnac[:, 0:1], scalar2=None,
                                                      op0=ALU.mult), deps=[t_ebx, t_lnc])
            c2_ = p.op("s", lambda s: s.activation(out=cw1[:, :], in_=cw1[:, :], func=AF.Exp),
                       deps=[c1_])
            c3_ = p.op("v", lambda v: v.tensor_tensor(out=cw2[:, :], in0=cw1[:, :], in1=mbx[:, :],
                                                      op=ALU.mult), deps=[c2_, t_mbx])
            c4_ = p.op("v", lambda v: v.tensor_tensor(out=cw2[:, :], in0=cw2[:, :], in1=smT[:, :],
                                                      op=ALU.mult), deps=[c3_, t_sm])
            t_cry = p.op("v", lambda v: v.tensor_reduce(out=carry[:, :], in_=cw2[:, :], axis=AX.X,
                                                        op=ALU.add), deps=[c4_])
            t_apw = p.op("s", lambda s: s.activation(out=ABC, in_=IOT, func=AF.Exp,
                                                     scale=lnac[:, 0:1]), deps=[t_scan, t_lnc])
            t_xs = p.op("v", lambda v: v.scalar_tensor_tensor(out=BU, in0=ABC, scalar=carry[:, 0:1],
                                                              in1=XS0, op0=ALU.mult, op1=ALU.add),
                        deps=[t_apw, t_cry, t_bu, t_scan])
            t_xsf = p.op("v", lambda v: v.tensor_copy(out=xsf, in_=BU), deps=[t_xs])

            if BIS <= 3.5:
                raise _StopEmission()
            t_tr3 = [[None] * ND for _ in range(NT)]
            t_x1sp = {}
            t_xm = []
            t_ty = {}
            t_tyn = {}
            t_tx1 = {}
            t_txn3 = {}
            for g in range(3):
                ku = 8 + g
                pv = 4 + g
                t_up[ku] = p.dmaop("y", lambda s, g=g: s.dma_start(
                    out=upr(8 + g).rearrange("p (w d) -> p w d", d=D),
                    in_=u_in[g * 256:(g + 1) * 256, :].rearrange("(w p) d -> p w d", p=128)),
                    deps=[t_xn1[2 * pv + 1], lst1[pv]], ch=UCH[(8 + g) % 4])

            def u2p(g):
                return upr(8 + g)

            def emitA(g):
                m2_sl, r2_sl, s2_sl, q2_sl = pair_slots(g, 1)[0:4]
                evw = {}
                for w in range(2):
                    i = 2 * g + w
                    last = None
                    for nh in range(2):
                        for c in range(4 * nh, 4 * nh + 4):
                            dps = [last] if last is not None else (
                                [t_dgq, t_xsf, t_ct, t_tr1[i][0]]
                                + ([t_evC[(g - 1, w)], t_evQ[(g - 1, w)]] if g >= 1 else []))
                            last = p.op("t", lambda t, g=g, w=w, c=c, nh=nh, i=i: t.matmul(
                                ps_y[:, w * D + nh * 512:w * D + (nh + 1) * 512],
                                lhsT=XT[:, c * T + i * 128:c * T + (i + 1) * 128],
                                rhs=dgq[:, c * 512:(c + 1) * 512],
                                start=(c == 4 * nh), stop=False), deps=dps)
                        last = p.op("t", lambda t, g=g, w=w, nh=nh, i=i: t.matmul(
                            ps_y[:, w * D + nh * 512:w * D + (nh + 1) * 512],
                            lhsT=xsf[:, i * 128:(i + 1) * 128],
                            rhs=ct16[:, nh * 512:(nh + 1) * 512],
                            start=False, stop=True), deps=[last])
                    t_dgl[(g, w)] = last
                    evw[w] = last
                for w in range(2):
                    t_evC[(g, w)] = p.op("s", lambda s, g=g, w=w, s2_sl=s2_sl: s.activation(
                        out=yri(g)[:, w * D:(w + 1) * D], in_=ps_y[:, w * D:(w + 1) * D],
                        func=AF.Copy, accum_out=s2_sl[:, w:w + 1]),
                        deps=[evw[w]]
                        + ([t_tyn[(g - 2, w)]] if g >= 2 else [t_xs]))
                    tq = p.op("s", lambda s, g=g, w=w, q2_sl=q2_sl: s.activation(
                        out=sqs_a, in_=ps_y[:, w * D:(w + 1) * D],
                        func=AF.Square, accum_out=q2_sl[:, w:w + 1]),
                        deps=[evw[w]] + ([sqch[0]] if sqch[0] is not None else []))
                    sqch[0] = tq
                    t_evQ[(g, w)] = tq
                t3 = p.op("v", lambda v, m2_sl=m2_sl, s2_sl=s2_sl: v.tensor_scalar(
                    out=m2_sl, in0=s2_sl, scalar1=1.0 / D, scalar2=None, op0=ALU.mult),
                    deps=[t_evC[(g, 0)], t_evC[(g, 1)]])
                t4 = p.op("v", lambda v, r2_sl=r2_sl, m2_sl=m2_sl: v.tensor_tensor(
                    out=r2_sl, in0=m2_sl, in1=m2_sl, op=ALU.mult), deps=[t3])
                t5 = p.op("v", lambda v, r2_sl=r2_sl, q2_sl=q2_sl: v.scalar_tensor_tensor(
                    out=r2_sl, in0=q2_sl, scalar=1.0 / D, in1=r2_sl, op0=ALU.mult,
                    op1=ALU.subtract), deps=[t_evQ[(g, 0)], t_evQ[(g, 1)], t4])
                t6 = p.op("s", lambda s, r2_sl=r2_sl: s.activation(out=r2_sl, in_=r2_sl,
                                                                   func=AF.Ln, bias=cb[:, 0:1]),
                          deps=[t5, t_cb])
                t_st2[g] = p.op("s", lambda s, r2_sl=r2_sl: s.activation(
                    out=r2_sl, in_=r2_sl, func=AF.Exp, scale=-0.5), deps=[t6])

            def emitB1(g):
                m2_sl, r2_sl = pair_slots(g, 1)[0:2]
                for w in range(2):
                    t_tyn[(g, w)] = p.op("v", lambda v, g=g, w=w, m2_sl=m2_sl, r2_sl=r2_sl:
                                         v.tensor_scalar(
                        out=ynri(g)[:, w * D:(w + 1) * D], in0=yri(g)[:, w * D:(w + 1) * D],
                        scalar1=m2_sl[:, w:w + 1], scalar2=r2_sl[:, w:w + 1],
                        op0=ALU.subtract, op1=ALU.mult),
                        deps=[t_st2[g]] + ([t_tx1[g - 2]] if g >= 2 else []))
                t_tx1[g] = p.op("v", lambda v, g=g: v.tensor_tensor(
                    out=x1t(g), in0=u2p(g), in1=ynri(g), op=ALU.add),
                    deps=[t_tyn[(g, 0)], t_tyn[(g, 1)], t_up[8 + g]]
                    + ([t_x1sp[g - 2], t_txn3[g - 2]] if g >= 2 else []))
                if g + 3 < NP2:
                    ku = 8 + g + 3
                    pv = ku - 4
                    dd = [t_xn1[2 * pv + 1], lst1[pv]] if pv < 8 else [t_tx1[pv - 8]]
                    t_up[ku] = p.dmaop("y", lambda s, g=g, ku=ku: s.dma_start(
                        out=upr(ku).rearrange("p (w d) -> p w d", d=D),
                        in_=u_in[(g + 3) * 256:(g + 4) * 256, :]
                            .rearrange("(w p) d -> p w d", p=128),
                        ), deps=dd, ch=UCH[ku % 4])

            def emitB2n(g):
                m1_sl = pair_slots(g, 0)[0]
                r3_sl, s3_sl, q3_sl = pair_slots(g, 2)[1:4]
                tqa = p.op("s", lambda s, g=g, q3_sl=q3_sl: s.activation(
                    out=sqs_a, in_=x1t(g)[:, 0:D], func=AF.Square, accum_out=q3_sl[:, 0:1]),
                    deps=[t_tx1[g]] + ([sqch[0]] if sqch[0] is not None else []))
                tqb = p.op("s", lambda s, g=g, q3_sl=q3_sl: s.activation(
                    out=sqs_a, in_=x1t(g)[:, D:2 * D], func=AF.Square, accum_out=q3_sl[:, 1:2]),
                    deps=[t_tx1[g], tqa])
                sqch[0] = tqb
                t4 = p.op("v", lambda v, r3_sl=r3_sl, m1_sl=m1_sl: v.tensor_tensor(
                    out=r3_sl, in0=m1_sl, in1=m1_sl, op=ALU.mult), deps=[t_tx1[g]])
                t5 = p.op("v", lambda v, r3_sl=r3_sl, q3_sl=q3_sl: v.scalar_tensor_tensor(
                    out=r3_sl, in0=q3_sl, scalar=1.0 / D, in1=r3_sl, op0=ALU.mult,
                    op1=ALU.subtract), deps=[tqb, t4])
                t6 = p.op("s", lambda s, r3_sl=r3_sl: s.activation(out=r3_sl, in_=r3_sl,
                                                                   func=AF.Ln, bias=cb[:, 0:1]),
                          deps=[t5, t_cb])
                tst3 = p.op("s", lambda s, r3_sl=r3_sl: s.activation(
                    out=r3_sl, in_=r3_sl, func=AF.Exp, scale=-0.5), deps=[t6])
                for w in range(2):
                    war_e = ([t_tr3[2 * (g - 2) + w][0], t_xm[g - 2]] if g >= 2 else [])
                    t3w = p.op("v", lambda v, g=g, w=w, m1_sl=m1_sl, r3_sl=r3_sl:
                               v.tensor_scalar(
                        out=xn3r(g)[:, w * D:(w + 1) * D], in0=x1t(g)[:, w * D:(w + 1) * D],
                        scalar1=m1_sl[:, w:w + 1], scalar2=r3_sl[:, w:w + 1],
                        op0=ALU.subtract, op1=ALU.mult), deps=[tst3] + war_e)
                    t_tw3[(g, w)] = t3w
                    t_txn3[g] = t3w
                t_xm.append(p.op("v", lambda v, g=g: v.tensor_reduce(
                    out=xmax[:, 2 * g:2 * g + 2],
                    in_=xn3r(g).rearrange("p (w d) -> p w d", d=D),
                    axis=AX.X, op=ALU.max, apply_absolute_value=True),
                    deps=[t_txn3[g]]))
                t_x1sp[g] = p.dmaop("y", lambda s, g=g: s.dma_start(
                    out=x1_sp[g * 256:(g + 1) * 256, :]
                        .rearrange("(w p) d -> p w d", p=128),
                    in_=x1t(g).rearrange("p (w d) -> p w d", d=D)),
                    deps=[t_tx1[g], tst3], ch="sp")

            def emitB2(g):
                for w in range(2):
                    i = 2 * g + w
                    dstv3 = XT[:, :].rearrange("p (c t) -> p c t", t=T)[:, :, i * 128:(i + 1) * 128]
                    ev3 = p.dmaop("y", lambda s, g=g, w=w, dstv3=dstv3: s.dma_start(
                        out=dstv3, in_=xn3r(g)[:, w * D:(w + 1) * D], transpose=True),
                        deps=[t_tw3[(g, w)], t_bu_mm[-1], t_dgl[(g, w)]],
                        ch="tr0" if w == 0 else "tr1")
                    for c in range(ND):
                        t_tr3[i][c] = ev3

            t_st2 = {}
            t_evC = {}
            t_evQ = {}
            t_dgl = {}
            t_tw3 = {}
            emitA(0)
            emitA(1)
            emitB1(0)
            emitA(2)
            emitB1(1)
            emitB2n(0)
            p.wait("y", _agw[0])
            t_wgr = []
            for g in range(3, NP2):
                emitA(g)
                emitB2(g - 3)
                t_wgr.append(w_reload(0, g - 3, dq_last))
                emitB1(g - 1)
                emitB2n(g - 2)
            emitB2(NP2 - 3)
            t_wgr.append(w_reload(0, 5, dq_last))
            emitB1(NP2 - 1)
            emitB2n(NP2 - 2)
            emitB2(NP2 - 2)
            t_wgr.append(w_reload(0, 6, dq_last))
            emitB2n(NP2 - 1)
            emitB2(NP2 - 1)
            t_wgr.append(w_reload(0, 7, dq_last))

            if BIS <= 4:
                raise _StopEmission()
            tmx = cross_max(xmax[:, :], t_xm, 0, 0)
            a0s = p.dmaop("g", lambda g: g.dma_start(out=ar_in[0][0:1, 0:1], in_=red0[0:1, 0:1]),
                          deps=[tmx], ch="gs")
            a0 = p.coll("AllReduce", ALU.max, [ar_in[0][:, 0:1]], [ar_out[0][:, 0:1]], deps=[a0s])
            t_mx = p.dmaop("g", lambda g: g.dma_start(out=SC[0:1, 0:1], in_=ar_out[0][0:1, 0:1]),
                           deps=[a0], ch="gs")
            t_sx = p.op("v", lambda v: v.tensor_scalar(out=SC[0:1, 1:2], in0=SC[0:1, 0:1],
                                                       scalar1=QI, scalar2=1e-12, op0=ALU.mult,
                                                       op1=ALU.max), deps=[t_mx])
            t_c1 = p.op("v", lambda v: v.reciprocal(out=SC[0:1, 2:3], in_=SC[0:1, 1:2]), deps=[t_sx])
            bm0 = p.op("t", lambda t: t.matmul(
                ps_y[:, 0:1], lhsT=onesr[:, :], rhs=SC[0:1, 2:3], start=True, stop=True),
                deps=[t_c1, t_onesr, t_evQ[(NP2 - 1, 0)], t_evQ[(NP2 - 1, 1)],
                      t_evC[(NP2 - 1, 0)], t_evC[(NP2 - 1, 1)]])
            t_bc1 = p.op("s", lambda s: s.activation(out=BC[:, 0:1], in_=ps_y[:, 0:1],
                                                     func=AF.Copy), deps=[bm0])

            t_xq = []
            for c in range(ND):
                sc_ = gir(0)
                deps0 = [t_tr3[i][c] for i in range(NT)] + [t_bc1] + t_xq[-1:]
                ta = p.op("v", lambda v, c=c, sc_=sc_: v.tensor_scalar(
                    out=sc_, in0=XT[:, c * T:(c + 1) * T], scalar1=BC[:, 0:1], scalar2=None,
                    op0=ALU.mult), deps=deps0)
                tb = p.op("v", lambda v, c=c, sc_=sc_: v.tensor_copy(
                    out=XT[:, c * T:(c + 1) * T], in_=sc_), deps=[ta])
                t_xq.append(tb)

            if BIS <= 5:
                raise _StopEmission()
            ps_pre.close()
            ps_mm = [st.enter_context(nc.psum_tensor(f"ps_mm{k}", [128, T], F32)) for k in range(2)]

            if BIS <= 5.2:
                raise _StopEmission()
            NOg = NO if BIS > 5.5 else 2
            if BIS <= 5.45:
                NOg = 2

            t_gev = {}
            t_gmx = {}
            last = None
            for oc in range(NOg):
                psum = ps_mm[oc % 2]
                war = [t_gev[oc - 2], t_gmx[oc - 2]] if oc >= 2 else []
                for c in range(ND):
                    for tq in range(4):
                        dps = ([t_xq[c], t_wgr[c]] + war) if tq == 0 else []
                        last = p.op("t", lambda t, oc=oc, c=c, tq=tq, psum=psum: t.matmul(
                            psum[:, tq * 512:(tq + 1) * 512],
                            lhsT=WBUF[:, c * H + oc * 128:c * H + (oc + 1) * 128],
                            rhs=XT[:, c * T + tq * 512:c * T + (tq + 1) * 512],
                            start=(c == 0), stop=(c == ND - 1)), deps=dps + ([last] if last else []))
                war_ar = [t_x1sp[NP2 - 1]]
                if BIS <= 5.3:
                    t_gev[oc] = p.op("v", lambda v, oc=oc, psum=psum: v.tensor_copy(
                        out=big(oc), in_=psum[:, :]), deps=[last] + war_ar)
                    t_gmx[oc] = t_gev[oc]
                    continue
                t_gev[oc] = p.op("s", lambda s, oc=oc, psum=psum: s.activation(
                    out=big(oc), in_=psum[:, :], func=AF.Copy, scale=RAWS),
                    deps=[last] + war_ar)
                if BIS <= 5.4:
                    t_gmx[oc] = t_gev[oc]
                    continue
                t_gmx[oc] = p.op("v", lambda v, oc=oc: v.tensor_reduce(
                    out=gmax[:, oc:oc + 1], in_=big(oc), axis=AX.X, op=ALU.max,
                    apply_absolute_value=True), deps=[t_gev[oc]])

            if BIS <= 5.5:
                raise _StopEmission()
            tg = cross_max(gmax[:, :], [t_gmx[oc] for oc in range(NO)], 1, 1)
            a1s = p.dmaop("g", lambda g: g.dma_start(out=ar_in[1][0:1, 0:1], in_=red0[0:1, 1:2]),
                          deps=[tg], ch="gs")
            a1 = p.coll("AllReduce", ALU.max, [ar_in[1][:, 0:1]], [ar_out[1][:, 0:1]], deps=[a1s])
            t_mg = p.dmaop("g", lambda g: g.dma_start(out=SC[0:1, 3:4], in_=ar_out[1][0:1, 0:1]),
                           deps=[a1], ch="gs")
            q1 = p.op("v", lambda v: v.tensor_tensor(out=SC[0:1, 16:17], in0=SC[0:1, 3:4],
                                                     in1=SC[0:1, 1:2], op=ALU.mult),
                      deps=[t_mg, t_sx])
            t_sg = p.op("v", lambda v: v.tensor_scalar(out=SC[0:1, 4:5], in0=SC[0:1, 16:17],
                                                       scalar1=QI / RAWS, scalar2=1e-12, op0=ALU.mult,
                                                       op1=ALU.max), deps=[q1])
            q2 = p.op("v", lambda v: v.reciprocal(out=SC[0:1, 17:18], in_=SC[0:1, 4:5]), deps=[t_sg])
            q3 = p.op("v", lambda v: v.tensor_tensor(out=SC[0:1, 18:19], in0=SC[0:1, 17:18],
                                                     in1=SC[0:1, 1:2], op=ALU.mult), deps=[q2])
            t_c2 = p.op("v", lambda v: v.tensor_scalar(out=SC[0:1, 5:6], in0=SC[0:1, 18:19],
                                                       scalar1=1.0 / RAWS, scalar2=None,
                                                       op0=ALU.mult), deps=[q3])
            b1s = p.dmaop("g", lambda g: g.dma_start(out=bb[1][0:1, 0:1], in_=SC[0:1, 4:5]),
                          deps=[t_sg], ch="gs")
            b1b = p.dmaop("g", lambda g: g.dma_start(out=bb[1][0:1, 1:2], in_=SC[0:1, 5:6]),
                          deps=[t_c2], ch="gs")
            t_bc2 = p.dmaop("g", lambda g: g.dma_start(
                out=BC[:, 1:3], in_=bb[1][0:1, 0:2].broadcast_to([128, 2])), deps=[b1s, b1b], ch="gs")

            if BIS <= 6:
                raise _StopEmission()
            p.wait("y", t_agw2[0])
            t_wur = [w_reload(1, c, [t_gev[NO - 1], t_gmx[NO - 1]]) for c in range(ND)]
            t_uev = {}
            t_umx = {}
            t_usp = {}
            t_g2 = {}
            t_g2m = {}
            for oc in range(NO):
                psum = ps_mm[oc % 2]
                war = ([t_uev[oc - 2], t_umx[oc - 2]] if oc >= 2
                       else [t_gev[NO - 2 + oc], t_gmx[NO - 2 + oc]])
                for c in range(ND):
                    for tq in range(4):
                        dps = ([t_wur[c]] + war) if tq == 0 else []
                        last = p.op("t", lambda t, oc=oc, c=c, tq=tq, psum=psum: t.matmul(
                            psum[:, tq * 512:(tq + 1) * 512],
                            lhsT=WBUF[:, c * H + oc * 128:c * H + (oc + 1) * 128],
                            rhs=XT[:, c * T + tq * 512:c * T + (tq + 1) * 512],
                            start=(c == 0), stop=(c == ND - 1)), deps=dps + [last])
                war1 = [t_g2[oc - 2]] if oc >= 2 else t_xq[-1:]
                ta = p.op("s", lambda s, oc=oc: s.activation(
                    out=gir(oc), in_=big(oc), func=AF.Copy, scale=BC[:, 2:3]),
                    deps=[t_gev[oc], t_bc2] + war1)
                tb = p.op("s", lambda s, oc=oc: s.activation(
                    out=sgr(oc), in_=gir(oc), func=AF.Sigmoid, scale=BC[:, 1:2]),
                    deps=[ta, t_bc2] + ([t_g2[oc - 2]] if oc >= 2 else []))
                tc = p.op("v", lambda v, oc=oc: v.tensor_scalar(
                    out=sur(oc), in0=sgr(oc), scalar1=65536.0, scalar2=None,
                    op0=ALU.mult), deps=[tb] + ([t_g2[oc - 2]] if oc >= 2 else []))
                tcf = p.op("v", lambda v, oc=oc: v.tensor_scalar(
                    out=sgr(oc), in0=sur(oc), scalar1=1.0 / 65536.0, scalar2=None,
                    op0=ALU.mult), deps=[tc])
                td = p.op("v", lambda v, oc=oc: v.tensor_tensor(
                    out=big(oc), in0=gir(oc), in1=sgr(oc), op=ALU.mult),
                    deps=[tcf, ta, t_gmx[oc]])
                t_g2[oc] = td
                t_g2m[oc] = p.op("v", lambda v, oc=oc: v.tensor_reduce(
                    out=g2m[:, oc:oc + 1], in_=big(oc), axis=AX.X, op=ALU.max,
                    apply_absolute_value=True), deps=[td])
                war_ev = [t_usp[oc - 1], t_umx[oc - 1]] if oc >= 1 else []
                t_uev[oc] = p.op("s", lambda s, oc=oc, psum=psum: s.activation(
                    out=evr(oc), in_=psum[:, :], func=AF.Copy, scale=RAWS),
                    deps=[last] + war_ev)
                t_umx[oc] = p.op("v", lambda v, oc=oc: v.tensor_reduce(
                    out=umax[:, oc:oc + 1], in_=evr(oc), axis=AX.X, op=ALU.max,
                    apply_absolute_value=True), deps=[t_uev[oc]])
                t_usp[oc] = p.dmaop("y", lambda s, oc=oc: s.dma_start(
                    out=up_sp[oc, :, :], in_=evr(oc)), deps=[t_uev[oc]], ch="us")

            tg2 = cross_max(g2m[:, :], [t_g2m[oc] for oc in range(NO)], 2, 2)
            tup = cross_max(umax[:, :], [t_umx[oc] for oc in range(NO)], 3, 3)
            a2s = p.dmaop("g", lambda g: g.dma_start(out=ar_in[2][0:1, 0:2], in_=red0[0:1, 2:4]),
                          deps=[tg2, tup], ch="gs")
            a2 = p.coll("AllReduce", ALU.max, [ar_in[2][:, 0:2]], [ar_out[2][:, 0:2]], deps=[a2s])
            t_m2u = p.dmaop("g", lambda g: g.dma_start(out=SC[0:1, 6:8], in_=ar_out[2][0:1, 0:2]),
                            deps=[a2], ch="gs")
            q4 = p.op("v", lambda v: v.tensor_tensor(out=SC[0:1, 19:20], in0=SC[0:1, 6:7],
                                                     in1=SC[0:1, 4:5], op=ALU.mult),
                      deps=[t_m2u, t_sg])
            t_s2 = p.op("v", lambda v: v.tensor_scalar(out=SC[0:1, 9:10], in0=SC[0:1, 19:20],
                                                       scalar1=QI, scalar2=1e-12, op0=ALU.mult,
                                                       op1=ALU.max), deps=[q4])
            q5 = p.op("v", lambda v: v.reciprocal(out=SC[0:1, 20:21], in_=SC[0:1, 9:10]), deps=[t_s2])
            t_c3 = p.op("v", lambda v: v.tensor_tensor(out=SC[0:1, 28:29], in0=SC[0:1, 20:21],
                                                       in1=SC[0:1, 4:5], op=ALU.mult), deps=[q5])
            q6 = p.op("v", lambda v: v.tensor_tensor(out=SC[0:1, 21:22], in0=SC[0:1, 7:8],
                                                     in1=SC[0:1, 1:2], op=ALU.mult),
                      deps=[t_m2u, t_sx])
            t_su = p.op("v", lambda v: v.tensor_scalar(out=SC[0:1, 10:11], in0=SC[0:1, 21:22],
                                                       scalar1=QI / RAWS, scalar2=1e-12, op0=ALU.mult,
                                                       op1=ALU.max), deps=[q6])
            q7 = p.op("v", lambda v: v.reciprocal(out=SC[0:1, 22:23], in_=SC[0:1, 10:11]),
                      deps=[t_su])
            q8 = p.op("v", lambda v: v.tensor_tensor(out=SC[0:1, 23:24], in0=SC[0:1, 22:23],
                                                     in1=SC[0:1, 1:2], op=ALU.mult), deps=[q7])
            t_cu = p.op("v", lambda v: v.tensor_scalar(out=SC[0:1, 29:30], in0=SC[0:1, 23:24],
                                                       scalar1=1.0 / RAWS, scalar2=None,
                                                       op0=ALU.mult), deps=[q8])
            bm2 = p.op("t", lambda t: t.matmul(
                ps_mm[0][:, 0:2], lhsT=onesr[:, :], rhs=SC[0:1, 28:30], start=True, stop=True),
                deps=[t_c3, t_cu, t_onesr, t_uev[NO - 2], t_umx[NO - 2]])
            t_bc3 = p.op("s", lambda s: s.activation(out=BC[:, 3:5], in_=ps_mm[0][:, 0:2],
                                                     func=AF.Copy), deps=[bm2])

            if BIS <= 7:
                raise _StopEmission()
            up_mm_done = last
            t_url = {}
            t_prod = {}
            t_pm = {}
            t_ui = {}
            for oc in range(NO):
                war_rl = [t_prod[oc - 2]] if oc >= 2 else [up_mm_done]
                t_url[oc] = p.dmaop("y", lambda s, oc=oc: s.dma_start(
                    out=rlr(oc), in_=up_sp[oc, :, :]), deps=[t_usp[oc]] + war_rl, ch="ur0" if oc % 2 == 0 else "ur1")
                war_gn = [t_prod[oc - 2]] if oc >= 2 else [up_mm_done]
                tgn = p.op("s", lambda s, oc=oc: s.activation(
                    out=gnr(oc), in_=big(oc), func=AF.Copy, scale=BC[:, 3:4]),
                    deps=[t_g2m[oc], t_bc3] + war_gn)
                tui = p.op("s", lambda s, oc=oc: s.activation(
                    out=uir(oc), in_=rlr(oc), func=AF.Copy, scale=BC[:, 4:5]),
                    deps=[t_url[oc], t_bc3] + war_gn)
                t_ui[oc] = tui
                tuf = p.op("v", lambda v, oc=oc: v.tensor_scalar(
                    out=rlr(oc), in0=uir(oc), scalar1=1.0 / 65536.0, scalar2=None,
                    op0=ALU.mult), deps=[tui])
                t_prod[oc] = p.op("v", lambda v, oc=oc: v.tensor_tensor(
                    out=big(oc), in0=gnr(oc), in1=rlr(oc), op=ALU.mult),
                    deps=[tgn, tuf])
                t_pm[oc] = p.op("v", lambda v, oc=oc: v.tensor_reduce(
                    out=pmax[:, oc:oc + 1], in_=big(oc), axis=AX.X, op=ALU.max,
                    apply_absolute_value=True), deps=[t_prod[oc]])

            tpp = cross_max(pmax[:, :], [t_pm[oc] for oc in range(NO)], 4, 4)
            a3s = p.dmaop("g", lambda g: g.dma_start(out=ar_in[3][0:1, 0:1], in_=red0[0:1, 4:5]),
                          deps=[tpp], ch="gs")
            a3 = p.coll("AllReduce", ALU.max, [ar_in[3][:, 0:1]], [ar_out[3][:, 0:1]], deps=[a3s])
            t_mp = p.dmaop("g", lambda g: g.dma_start(out=SC[0:1, 12:13], in_=ar_out[3][0:1, 0:1]),
                           deps=[a3], ch="gs")
            k1 = p.op("v", lambda v: v.tensor_tensor(out=SC[0:1, 24:25], in0=SC[0:1, 9:10],
                                                     in1=SC[0:1, 10:11], op=ALU.mult),
                      deps=[t_s2, t_su])
            k2 = p.op("v", lambda v: v.tensor_tensor(out=SC[0:1, 25:26], in0=SC[0:1, 24:25],
                                                     in1=SC[0:1, 12:13], op=ALU.mult),
                      deps=[k1, t_mp])
            t_s3 = p.op("v", lambda v: v.tensor_scalar(out=SC[0:1, 14:15], in0=SC[0:1, 25:26],
                                                       scalar1=65536.0 * QI, scalar2=1e-12,
                                                       op0=ALU.mult, op1=ALU.max), deps=[k2])
            k3 = p.op("v", lambda v: v.reciprocal(out=SC[0:1, 26:27], in_=SC[0:1, 14:15]),
                      deps=[t_s3])
            k4 = p.op("v", lambda v: v.tensor_tensor(out=SC[0:1, 27:28], in0=SC[0:1, 26:27],
                                                     in1=SC[0:1, 24:25], op=ALU.mult),
                      deps=[k3, k1])
            t_c4 = p.op("v", lambda v: v.tensor_scalar(out=SC[0:1, 13:14], in0=SC[0:1, 27:28],
                                                       scalar1=65536.0, scalar2=None,
                                                       op0=ALU.mult), deps=[k4])
            bm3 = p.op("t", lambda t: t.matmul(
                ps_mm[1][:, 0:2], lhsT=onesr[:, :], rhs=SC[0:1, 13:15], start=True, stop=True),
                deps=[t_c4, t_s3, t_onesr, t_uev[NO - 1], t_umx[NO - 1]])
            t_bc4 = p.op("s", lambda s: s.activation(out=BC[:, 5:7], in_=ps_mm[1][:, 0:2],
                                                     func=AF.Copy), deps=[bm3])

            if BIS <= 8:
                raise _StopEmission()
            t_of = {}
            for oc in range(NO):
                war = [t_of[oc - 2]] if oc >= 2 else [t_prod[NO - 1], t_pm[NO - 1], t_ui[NO - 1]]
                if oc % 2 == 0:
                    ta = p.op("s", lambda s, oc=oc: s.activation(
                        out=oir(oc), in_=big(oc), func=AF.Copy, scale=BC[:, 5:6]),
                        deps=[t_pm[oc], t_bc4] + war)
                else:
                    ta = p.op("v", lambda v, oc=oc: v.tensor_scalar(
                        out=oir(oc), in0=big(oc), scalar1=BC[:, 5:6], scalar2=None,
                        op0=ALU.mult), deps=[t_pm[oc], t_bc4] + war)
                t_of[oc] = p.op("v", lambda v, oc=oc: v.tensor_copy(out=big(oc), in_=oir(oc)),
                                deps=[ta])

            t_wdr = [w_reload(2, c, [up_mm_done]) for c in range(NO)]
            t_x1rl = {}
            t_out = {}
            t_odma = {}
            t_dnev = {}
            lsq4 = {}
            for g in range(NP2):
                war_x1 = [t_out[(g - 2, 0)], t_out[(g - 2, 1)]] if g >= 2 else []
                t_x1rl[g] = p.dmaop("y", lambda s, g=g: s.dma_start(
                    out=x1p(g).rearrange("p (w d) -> p w d", d=D),
                    in_=x1_sp[g * 256:(g + 1) * 256, :].rearrange("(w p) d -> p w d", p=128)),
                    deps=[t_x1sp[g], t_g2[NO - 1], t_usp[NO - 1], t_umx[NO - 1]] + war_x1,
                    ch="xr")
                psd = ps_mm[g % 2]
                for w in range(2):
                    i = 2 * g + w
                    for oc in range(NO):
                        for nh in range(2):
                            dps = []
                            if nh == 0:
                                dps = [t_of[oc], t_wdr[oc]]
                                if oc == 0 and w == 0:
                                    dps += ([t_dnev[(g - 2, 0)], t_dnev[(g - 2, 1)]] if g >= 2
                                            else [t_uev[NO - 2 + g], t_umx[NO - 2 + g]])
                            last = p.op("t", lambda t, oc=oc, nh=nh, i=i, w=w, psd=psd: t.matmul(
                                psd[:, w * D + nh * 512:w * D + (nh + 1) * 512],
                                lhsT=big(oc)[:, i * 128:(i + 1) * 128],
                                rhs=WBUF[:, oc * D + nh * 512:oc * D + (nh + 1) * 512],
                                start=(oc == 0), stop=(oc == NO - 1)), deps=dps + [last])
                m_sl, r_sl, s_sl, q_sl, b_sl = pair_slots(g, 3)
                war_y2 = ([t_t8[(g - 2, 0)], t_t8[(g - 2, 1)], lsq4[(g - 2, 1)],
                           t_odma[(g - 2, 0)], t_odma[(g - 2, 1)]] if g >= 2
                          else [t_prod[NO - 1]])
                tevs = []
                tsqs = []
                for w in range(2):
                    tev = p.op("s", lambda s, g=g, w=w, psd=psd, s_sl=s_sl: s.activation(
                        out=y2p(g)[:, w * D:(w + 1) * D], in_=psd[:, w * D:(w + 1) * D],
                        func=AF.Copy, scale=BC[:, 6:7], accum_out=s_sl[:, w:w + 1]),
                        deps=[last, t_bc4] + war_y2 + tevs)
                    t_dnev[(g, w)] = tev
                    tevs.append(tev)
                    war_sq = [lsq4[(g, 0)]] if w == 1 else ([lsq4[(g - 1, 1)]] if g >= 1
                                                            else last_sq[-1:])
                    tsq = p.op("s", lambda s, g=g, w=w, q_sl=q_sl: s.activation(
                        out=sqs_n, in_=y2p(g)[:, w * D:(w + 1) * D], func=AF.Square,
                        accum_out=q_sl[:, w:w + 1]),
                        deps=[tev, t_g2[NO - 1]] + war_sq)
                    lsq4[(g, w)] = tsq
                    tsqs.append(tsq)
                t3 = p.op("v", lambda v, m_sl=m_sl, s_sl=s_sl: v.tensor_scalar(
                    out=m_sl, in0=s_sl, scalar1=1.0 / D, scalar2=None, op0=ALU.mult),
                    deps=tevs)
                t4 = p.op("v", lambda v, r_sl=r_sl, m_sl=m_sl: v.tensor_tensor(
                    out=r_sl, in0=m_sl, in1=m_sl, op=ALU.mult), deps=[t3])
                t5 = p.op("v", lambda v, r_sl=r_sl, q_sl=q_sl: v.scalar_tensor_tensor(
                    out=r_sl, in0=q_sl, scalar=1.0 / D, in1=r_sl, op0=ALU.mult,
                    op1=ALU.subtract), deps=tsqs + [t4])
                t6 = p.op("s", lambda s, r_sl=r_sl: s.activation(out=r_sl, in_=r_sl, func=AF.Ln,
                                                                 bias=cb[:, 0:1]), deps=[t5, t_cb])
                t7 = p.op("s", lambda s, r_sl=r_sl: s.activation(out=r_sl, in_=r_sl, func=AF.Exp,
                                                                 scale=-0.5), deps=[t6])
                if g == 0:
                    t_t8 = {}
                for w in range(2):
                    war_yn = ([t_out[(g - 2, 0)], t_out[(g - 2, 1)]] if g >= 2
                              else [t_prod[NO - 1]])
                    t_t8[(g, w)] = p.op("v", lambda v, g=g, w=w, m_sl=m_sl, r_sl=r_sl: v.tensor_scalar(
                        out=yn4p(g)[:, w * D:(w + 1) * D], in0=y2p(g)[:, w * D:(w + 1) * D],
                        scalar1=m_sl[:, w:w + 1], scalar2=r_sl[:, w:w + 1],
                        op0=ALU.subtract, op1=ALU.mult),
                        deps=[t7] + war_yn)
                for w in range(2):
                    t_out[(g, w)] = p.op("v", lambda v, g=g, w=w: v.tensor_tensor(
                        out=outp(g)[:, w * D:(w + 1) * D], in0=x1p(g)[:, w * D:(w + 1) * D],
                        in1=yn4p(g)[:, w * D:(w + 1) * D], op=ALU.add),
                        deps=[t_t8[(g, w)], t_x1rl[g], lsq4[(g, w)]])
                    t_odma[(g, w)] = p.dmaop("y", lambda s, g=g, w=w: s.dma_start(
                        out=out_d[g * 256 + w * 128:g * 256 + (w + 1) * 128, :],
                        in_=outp(g)[:, w * D:(w + 1) * D]),
                        deps=[t_out[(g, w)]], ch="ot" if w == 0 else "ot2")

            p.wait("y", t_odma[(NP2 - 1, 0)])
            p.wait("y", t_odma[(NP2 - 1, 1)])
        except _StopEmission:
            tz = p.op("g", lambda g: g.memset(AR32[:, 0:D], 0.0))
            tzs = []
            for i in range(NT):
                tzs.append(p.dmaop("y", lambda s, i=i: s.dma_start(
                    out=out_d[i * 128:(i + 1) * 128, :], in_=AR32[:, 0:D]),
                    deps=[tz], ch="ot"))
            p.wait("y", tzs[-1])

        @block.sync
        def _(eng):
            for f in p.ops["y"]:
                f(eng)

        @block.vector
        def _(eng):
            for f in p.ops["v"]:
                f(eng)

        @block.scalar
        def _(eng):
            for f in p.ops["s"]:
                f(eng)

        @block.gpsimd
        def _(eng):
            for f in p.ops["g"]:
                f(eng)

        @block.tensor
        def _(eng):
            for f in p.ops["t"]:
                f(eng)

    return nc


def kernel(L0_attn_u, L0_attn_log_A_real, L0_attn_B, L0_attn_C, L0_attn_D,
           L0_attn_log_delta, L0_ffn_w_gate, L0_ffn_w_up, L0_ffn_w_down):
    u = np.ascontiguousarray(np.asarray(L0_attn_u, np.float32).reshape(SEQ, D))
    ldr = np.ascontiguousarray(np.asarray(L0_attn_log_delta, np.float32).reshape(8, 128).T)
    dvf = np.asarray(L0_attn_D, np.float32)
    dgm = np.zeros((128, 4096), np.float32)
    for c in range(8):
        dgm[np.arange(128), c * 512 + (c % 4) * 128 + np.arange(128)] = dvf[c * 128:c * 128 + 128]
    dgm = np.ascontiguousarray(dgm)
    lar = np.ascontiguousarray(np.asarray(L0_attn_log_A_real, np.float32))
    btr = np.ascontiguousarray(np.asarray(L0_attn_B, np.float32).T)
    ctr = np.ascontiguousarray(np.asarray(L0_attn_C, np.float32).T)
    wg = np.ascontiguousarray(np.asarray(L0_ffn_w_gate, np.float32))
    wu = np.ascontiguousarray(np.asarray(L0_ffn_w_up, np.float32))
    wd = np.ascontiguousarray(np.asarray(L0_ffn_w_down, np.float32))

    if "nc" not in _CACHED:
        _CACHED["nc"] = build()
    nc = _CACHED["nc"]

    in_maps = []
    for k in range(NCORE):
        cexp = np.array([max(T * (k - 1 - m), 0) for m in range(8)], np.float32)
        cmask = np.array([1.0 if m < k else 0.0 for m in range(8)], np.float32)
        in_maps.append({
            "u_in": u[k * T:(k + 1) * T],
            "ldr": ldr, "dgm": dgm, "lar": lar, "btr": btr, "ctr": ctr,
            "wg_nat": wg[k * OG:(k + 1) * OG], "wu_nat": wu[k * OG:(k + 1) * OG],
            "wd_nat": wd[k * HG:(k + 1) * HG],
            "cexp": cexp, "cmask": cmask,
            "idm": np.eye(128, dtype=np.float16),
        })
    res = run_bass_kernel_spmd(nc, in_maps, core_ids=list(range(NCORE)))
    out = np.concatenate([np.asarray(r["out_d"], np.float32) for r in res.results], axis=0)
    return out.reshape(1, SEQ, D)
